# version 05
# baseline (speedup 1.0000x reference)
import contextlib
import numpy as np

import concourse.bass as bass
import concourse.mybir as mybir
from concourse.bass_utils import run_bass_kernel_spmd

F32 = mybir.dt.float32
BF16 = mybir.dt.bfloat16
AF = mybir.ActivationFunctionType
ALU = mybir.AluOpType
AXX = mybir.AxisListType.X

T = 2048
C = 1024
NH = 16
DH = 64
NSEQ = 2
NT = 16
NBLK = 4
NCH = 8
NHALF = 2 * NH * NT
EPS = 1e-5
SCALE = 0.125

DMA_GROUPS = {
    "d_x": 3, "d_y1l": 2, "d_qQ": 2, "d_vb": 1, "d_x2l": 2, "d_y2f": 1,
    "d_ypv": 2, "d_cbT": 1, "d_wh": 1, "d_kld": 1,
    "d_w": 1, "d_y1s": 4, "d_qv": 4, "d_x2s": 3, "d_y2s": 2, "d_out": 2,
    "d_z": 1,
}
CMP_SEMS = ("v_st", "v_ti", "v_nrm", "v_ev", "v_qn", "v_cb", "v_rs", "v_rsf",
            "v_vp", "v_x2", "v_mix", "v_ka", "v_kt", "v_rt", "v_fin", "v_n2",
            "v_set", "t_mm", "t_qn", "t_sc", "t_B", "t_op", "t_kf", "t_kv",
            "t_r", "t_rsT", "s_sq", "s_ex", "s_exd", "s_th", "g_io")


def dma_names():
    out = []
    for base, n in DMA_GROUPS.items():
        out += [f"{base}{i}" for i in range(n)]
    return out


class _StubIns:
    def then_inc(self, *a, **k):
        return self


class _Stub:
    def __getattr__(self, k):
        return lambda *a, **kw: _StubIns()


def build(debug=False):
    nc = bass.Bass("TRN2", target_bir_lowering=False)

    x_d = nc.dram_tensor("x", [NSEQ, T, C], F32, kind="ExternalInput").ap()
    w_in = {}
    for nm in ("wqT", "wkT", "wvT", "woT", "wrT", "wkfT", "wvfT"):
        w_in[nm] = nc.dram_tensor(nm, [C, C], F32, kind="ExternalInput").ap()
    vecs = {}
    for nm in ("bqc", "bkc", "bvc", "murc", "mukc"):
        vecs[nm] = nc.dram_tensor(nm, [128, 8], F32, kind="ExternalInput").ap()
    bob_d = nc.dram_tensor("bob", [128, C], F32, kind="ExternalInput").ap()
    out_d = nc.dram_tensor("out", [NSEQ, T, C], F32, kind="ExternalOutput").ap()

    kind_s = "ExternalOutput" if debug else "Internal"
    y1d = [nc.dram_tensor(f"y1d{s}", [T, C], BF16, kind=kind_s).ap() for s in range(NSEQ)]
    qd = [nc.dram_tensor(f"qd{s}", [C, T], BF16, kind=kind_s).ap() for s in range(NSEQ)]
    vd = [nc.dram_tensor(f"vd{s}", [C, T], BF16, kind=kind_s).ap() for s in range(NSEQ)]
    kd1 = nc.dram_tensor("kd1", [C, T], BF16, kind=kind_s).ap()
    x2d = [nc.dram_tensor(f"x2d{s}", [T, C], F32, kind=kind_s).ap() for s in range(NSEQ)]
    y2d = [nc.dram_tensor(f"y2d{s}", [T + 1, C], BF16, kind=kind_s).ap() for s in range(NSEQ)]

    tot = {}
    rec = {}
    sems = {}
    dry = [True]

    st = contextlib.ExitStack()
    with st:
        for nm in dma_names() + list(CMP_SEMS):
            sems[nm] = st.enter_context(nc.semaphore(nm))
            tot[nm] = 0

        def sb(name, shape, dt):
            return st.enter_context(nc.sbuf_tensor(name, shape, dt))

        def inc(ins, name, k=1):
            tot[name] += k
            ins.then_inc(sems[name], k)
            return tot[name]

        def dinc(eng, name, k=1):
            tot[name] += k
            eng.drain(fusable=True).then_inc(sems[name], k)
            return tot[name]

        def wge(eng, name, val):
            if val > 0:
                eng.wait_ge(sems[name], val)

        def wrec(eng, key):
            if key not in rec:
                assert dry[0], f"missing rec {key}"
                return
            nm, v = rec[key]
            wge(eng, nm, v)

        def rdma(key, ins, name):
            rec[key] = (name, inc(ins, name, 16))

        def rcmp(key, name):
            rec[key] = (name, tot[name])

        K_sb = sb("K_sb", [128, NCH * T], BF16)
        qblk = sb("qblk", [128, 2 * 1024], BF16)
        bob = sb("bob_sb", [128, C], F32)
        dump8 = sb("dump8", [128, 4 * 1024], mybir.dt.float8e4)
        sdump = sb("sdump", [128, 1024], BF16)
        cbias = sb("cbias", [128, NSEQ * NT * NH], F32)
        cbias_bf = sb("cbias_bf", [128, NSEQ * NT * NH], BF16)
        denom = sb("denom", [128, NSEQ * NHALF], F32)
        rs16 = sb("rs16", [128, NH], F32)
        rcp16 = sb("rcp16", [128, NH], F32)
        rs_f32 = sb("rs_f32", [128, NSEQ * NT * NH], F32)
        kmax2 = sb("kmax2", [16, 1], F32)
        negk16 = sb("negk16", [16, 1], F32)
        bvec = {nm: sb(nm + "_sb", [128, 8], F32) for nm in
                ("bqc", "bkc", "bvc", "murc", "mukc")}
        identi = sb("identi", [128, 128], F32)
        es2i = sb("es2i", [16, 128], F32)
        eji = sb("eji", [128, 16], F32)
        pg = sb("pg", [128, 1], F32)
        pi = sb("pi", [128, 1], F32)
        eqt = sb("eqt", [128, 16], F32)
        eselq = sb("eselq", [128, NCH * 16], BF16)
        epsv = sb("epsv", [128, 1], F32)
        sums = sb("sums", [128, NT], F32)
        ssqs = sb("ssqs", [128, NT], F32)
        m2t = sb("m2t", [128, NT], F32)
        meanv = sb("meanv", [128, NT], F32)
        varv = sb("varv", [128, NT], F32)
        stdv = sb("stdv", [128, NT], F32)
        invv = sb("invv", [128, NT], F32)
        negmi = sb("negmi", [128, NT], F32)
        rsf_sb = sb("rsf_sb", [16, 512], BF16)
        qkn_sb = sb("qkn_sb", [48, T], BF16)

        PB = 53248
        poolB = sb("poolB", [128, PB], BF16)
        WQ, WK, WV = 0, 8192, 16384
        Y1ROT, Y1F, QVROT, SQROT = 24576, 28672, 32768, 34816
        WO, WR, WKF, WVF = 0, 8192, 16384, 24576
        VBLK, Y2F, XKc, KACT, Y2ROT, YPV, ESEL2 = (32768, 36864, 40960, 45056,
                                                   49152, 51200, 52224)
        zrow = poolB[0:1, Y1ROT:Y1ROT + C]

        PF = 10240
        poolF = sb("poolF", [128, PF], F32)
        XROT = 0
        XLD, X2RD, KTMP, TMP2, RT = 0, 2048, 4096, 6144, 8192

        def pb(col, n, p0=0, p1=128):
            return poolB[p0:p1, col:col + n]

        def pf(col, n):
            return poolF[:, col:col + n]

        pS = st.enter_context(nc.psum_tensor("pS", [128, 2048], F32))
        pAB = st.enter_context(nc.psum_tensor("pAB", [128, 1024], F32))
        pD = st.enter_context(nc.psum_tensor("pD", [128, 1024], F32))

        def xa_load(sync, s, i):
            if s == 1 and i < 2:
                wrec(sync, ("v_nrm", 0, NT - 2 + i))
            if i >= 2:
                wrec(sync, ("v_st", s, i - 2))
            rdma(("xA", s, i),
                 sync.dma_start(pf(XROT + (i % 2) * C, C),
                                x_d[s, i * 128:(i + 1) * 128, :]),
                 f"d_x{i % 2}")

        def xb_load(sync, s, i):
            if i < 2:
                wrec(sync, ("v_st", s, NT - 1))
            else:
                wrec(sync, ("v_nrm", s, i - 2))
            rdma(("xB", s, i),
                 sync.dma_start(pf(XROT + (i % 2) * C, C),
                                x_d[s, i * 128:(i + 1) * 128, :]),
                 f"d_x{i % 2}")

        def y1f_load(sync, s, t):
            for j in range(4):
                wrec(sync, ("y1s", s, 4 * t + j))
            if s == 0:
                if t >= 1:
                    wrec(sync, ("t_mm", 0, 16 * t - 1))
            else:
                if t == 0:
                    wrec(sync, ("t_mm", 0, 63))
                else:
                    wrec(sync, ("t_mm", 1, 24 * t - 1))
            for cc in range(NCH):
                ins = sync.dma_start(
                    pb(Y1F + cc * 512, 512),
                    y1d[s][t * 512:(t + 1) * 512, cc * 128:(cc + 1) * 128],
                    transpose=True)
                inc(ins, "d_y1l0", 16)
            rcmp(("y1f", s, t), "d_y1l0")

        def y1fv_load(sync, t):
            if t == 0:
                wrec(sync, ("t_mm", 1, 24 * NBLK - 1))
            else:
                wrec(sync, ("t_mm", 0, 63 + 8 * t))
            for cc in range(NCH):
                ins = sync.dma_start(
                    pb(Y1F + cc * 512, 512),
                    y1d[0][t * 512:(t + 1) * 512, cc * 128:(cc + 1) * 128],
                    transpose=True)
                inc(ins, "d_y1l0", 16)
            rcmp(("y1fv", t), "d_y1l0")

        def cbt_load(sync, s):
            wrec(sync, ("v_cb", s))
            for lt in range(NT):
                ins = sync.dma_start(
                    cbias_bf[:, (s * NT + lt) * NH:(s * NT + lt + 1) * NH],
                    qkn_sb[0:16, lt * 128:(lt + 1) * 128], transpose=True)
                inc(ins, "d_cbT0", 16)
            rcmp(("cbT", s), "d_cbT0")

        def qq_load(sync, s, lt):
            if lt == 0 and s == 0:
                for sl in range(4):
                    wrec(sync, ("qs_all", sl))
            elif lt == 0:
                for sl in range(4):
                    wrec(sync, ("qv_all", s, sl))
            if lt >= 2:
                wrec(sync, ("t_sc_half", s, 32 * (lt - 1) - 1))
            for cc in range(NCH):
                ins = sync.dma_start(
                    qblk[:, (lt % 2) * 1024 + cc * 128:
                         (lt % 2) * 1024 + (cc + 1) * 128],
                    qd[s][cc * 128:(cc + 1) * 128, lt * 128:(lt + 1) * 128])
                inc(ins, f"d_qQ{lt % 2}", 16)
            rcmp(("qQ", s, lt), f"d_qQ{lt % 2}")

        def kld_load(sync, fc):
            wrec(sync, ("qv", 1, 80 + fc))
            wrec(sync, ("t_sc_half", 0, 483 + 4 * fc))
            ins = sync.dma_start(K_sb[:, fc * T:(fc + 1) * T],
                                 kd1[fc * 128:(fc + 1) * 128, :])
            rec[("Kld",)] = ("d_kld0", inc(ins, "d_kld0", 16))

        def vblk_load(sync, s, t):
            if s == 0 and t == 0:
                wrec(sync, ("p1_done_v", 1))
                wrec(sync, ("t_mm", 1, 24 * NBLK - 1))
                wrec(sync, ("t_qn", 1, 16 * NBLK - 1))
                for sl in range(4):
                    wrec(sync, ("qv_all0", sl))
            if t >= 1:
                wrec(sync, ("t_op", s, 4 * t - 1))
            elif s == 1:
                wrec(sync, ("t_op", 0, 4 * NBLK - 1))
            for cc in range(NCH):
                ins = sync.dma_start(
                    pb(VBLK + cc * 512, 512),
                    vd[s][cc * 128:(cc + 1) * 128, t * 512:(t + 1) * 512])
                inc(ins, "d_vb0", 16)
            rcmp(("vb", s, t), "d_vb0")

        def xld_load(sync, s, i):
            if i >= 2:
                wrec(sync, ("x2s", s, i - 2))
            elif s == 1:
                wrec(sync, ("v_x2", 0, NT - 1))
            rdma(("xld", s, i),
                 sync.dma_start(pf(XLD + (i % 2) * C, C),
                                x_d[s, i * 128:(i + 1) * 128, :]),
                 f"d_x{i % 2}")

        def x2l_load(sync, s, i):
            if i == 0:
                for sl in range(2):
                    wrec(sync, ("x2s_all", s, sl))
            if i >= 2:
                wrec(sync, ("v_n2", s, i - 2))
            elif s == 1:
                wrec(sync, ("v_fin", 0, NT - 2 + i))
            rdma(("x2l", s, i),
                 sync.dma_start(pf(X2RD + (i % 2) * C, C),
                                x2d[s][i * 128:(i + 1) * 128, :]),
                 f"d_x2l{i % 2}")

        def y2f_load(sync, s, t):
            if t == 0:
                for sl in range(2):
                    wrec(sync, ("y2s_all", s, sl))
                if s == 1:
                    wrec(sync, ("t_r", 0, 4 * NBLK - 1))
            else:
                wrec(sync, ("t_r", s, 4 * t - 1))
            for cc in range(NCH):
                ins = sync.dma_start(
                    pb(Y2F + cc * 512, 512),
                    y2d[s][1 + t * 512:1 + (t + 1) * 512,
                           cc * 128:(cc + 1) * 128],
                    transpose=True)
                inc(ins, "d_y2f0", 16)
            rcmp(("y2f", s, t), "d_y2f0")

        def ypv_load(sync, s, t, cc):
            if 8 * t + cc >= 2:
                wrec(sync, ("v_mix", s, 8 * t + cc - 2))
            elif s == 1:
                wrec(sync, ("v_mix", 0, 8 * NBLK - 2 + cc))
            ins = sync.dma_start(
                pb(YPV + (cc % 2) * 512, 512),
                y2d[s][t * 512:(t + 1) * 512, cc * 128:(cc + 1) * 128],
                transpose=True)
            rdma(("ypv", s, t, cc), ins, f"d_ypv{cc % 2}")

        def x2f_load(sync, s, e):
            i = NT + e
            if e >= 2:
                wrec(sync, ("v_fin", s, e - 2))
            else:
                wrec(sync, ("v_n2", s, NT - 1))
            rdma(("x2f", s, e),
                 sync.dma_start(pf(X2RD + (i % 2) * C, C),
                                x2d[s][e * 128:(e + 1) * 128, :]),
                 f"d_x2l{i % 2}")

        def f_sync(sync):
            for nm in ("bqc", "bkc", "bvc", "murc", "mukc"):
                inc(sync.dma_start(bvec[nm][:, :], vecs[nm][:, :]), "d_wh0", 16)
            inc(sync.dma_start(bob[:, :], bob_d[:, :]), "d_wh0", 16)
            rcmp(("wh",), "d_wh0")

            for i in range(NT):
                xa_load(sync, 0, i)
            for i in range(NT):
                xb_load(sync, 0, i)
            for t in range(NBLK):
                y1f_load(sync, 0, t)
            cbt_load(sync, 0)
            def emit_sorted(items):
                for _k, fn in sorted(items, key=lambda kv: kv[0]):
                    fn()

            items = [(-100.0, lambda: qq_load(sync, 0, 0)),
                     (-99.0, lambda: qq_load(sync, 0, 1))]
            for i in range(NT):
                items.append((-50.0 + i, lambda i=i: xa_load(sync, 1, i)))
            for i in range(NT):
                items.append((45.0 + i, lambda i=i: xb_load(sync, 1, i)))
            for lt in range(2, NT):
                items.append((32.0 * lt - 33,
                              lambda lt=lt: qq_load(sync, 0, lt)))
            for t in range(NBLK):
                k = max(pos_mark[("eM", 1, 24 * t)] - 6.0, 62.0)
                items.append((k, lambda t=t: y1f_load(sync, 1, t)))
            for t in range(NBLK):
                k = pos_mark[("eM", 0, 64 + 8 * t)] - 6.0
                items.append((k, lambda t=t: y1fv_load(sync, t)))
            items.append((pos_mark[("eM", 0, 95)] + 2.0,
                          lambda: cbt_load(sync, 1)))
            for fc in range(NCH):
                items.append((484.0 + 4 * fc, lambda fc=fc: kld_load(sync, fc)))
            emit_sorted(items)

            items = [(-100.0, lambda: qq_load(sync, 1, 0)),
                     (-99.0, lambda: qq_load(sync, 1, 1)),
                     (-98.0, lambda: vblk_load(sync, 0, 0))]
            for lt in range(2, NT):
                items.append((32.0 * lt - 33,
                              lambda lt=lt: qq_load(sync, 1, lt)))
            for t in range(1, NBLK):
                k = pos_mark[("eB", 0, t, 0)] - 512 - 10.0
                items.append((k, lambda t=t: vblk_load(sync, 0, t)))
            for i in range(NT):
                need = pos_mark[("eO", 0, min(i + 1, NT - 1))]
                if i % 4 == 3 and i // 4 + 1 < NBLK:
                    need = min(need, pos_mark[("eB", 0, i // 4 + 1, 0)])
                items.append((need - 512 - 10.0,
                              lambda i=i: xld_load(sync, 0, i)))
            fstart = pos_mark[("eF", 0, 0)] - 512
            for i in range(NT):
                items.append((fstart - 34.0 + 0.5 * i,
                              lambda i=i: x2l_load(sync, 0, i)))
            for t in range(NBLK):
                base = pos_mark[("eF", 0, 8 * t)] - 512
                if t == 0:
                    k = base - 10.0
                else:
                    k = base - 6.0
                items.append((k, lambda t=t: y2f_load(sync, 0, t)))
                for cc in range(NCH):
                    items.append((k + 0.5 + 0.1 * cc,
                                  lambda t=t, cc=cc: ypv_load(sync, 0, t, cc)))
            for e in range(NT):
                k = pos_mark[("eR", 0, min(e + 1, NT - 1))] - 512 - 12.0
                items.append((k, lambda e=e: x2f_load(sync, 0, e)))
            for t in range(NBLK):
                k = pos_mark[("eB", 1, t, 0)] - 512 - 10.0
                items.append((k, lambda t=t: vblk_load(sync, 1, t)))
            for i in range(NT):
                need = pos_mark[("eO", 1, min(i + 1, NT - 1))]
                if i % 4 == 3 and i // 4 + 1 < NBLK:
                    need = min(need, pos_mark[("eB", 1, i // 4 + 1, 0)])
                items.append((need - 512 - 10.0,
                              lambda i=i: xld_load(sync, 1, i)))
            emit_sorted(items)
            for i in range(NT):
                x2l_load(sync, 1, i)
            for t in range(NBLK):
                y2f_load(sync, 1, t)
                for cc in range(NCH):
                    ypv_load(sync, 1, t, cc)
                for j in range(4):
                    x2f_load(sync, 1, 4 * t + j)

        def f_gpsimd(g):
            for w_key, col in (("wqT", WQ), ("wkT", WK), ("wvT", WV)):
                for cc in range(NCH):
                    inc(g.dma_start(pb(col + cc * 1024, 1024),
                                    w_in[w_key][cc * 128:(cc + 1) * 128, :]),
                        "d_w0", 16)
            rcmp(("w1",), "d_w0")
            g.iota(identi[:, :], [[1, 128]], channel_multiplier=-1,
                   allow_small_or_imprecise_dtypes=True)
            g.iota(es2i[:, :], [[1, 2], [0, 64]], channel_multiplier=-1,
                   allow_small_or_imprecise_dtypes=True)
            g.iota(eji[:, :], [[1, 16]], channel_multiplier=0,
                   allow_small_or_imprecise_dtypes=True)
            ins = g.iota(pi[:, :], [[0, 1]], channel_multiplier=1,
                         allow_small_or_imprecise_dtypes=True)
            inc(ins, "g_io", 1)
            wge(g, "v_set", 1)
            for s in range(NSEQ):
                inc(g.dma_start(y2d[s][0:1, :], zrow), "d_z0", 16)
            rcmp(("zst",), "d_z0")

            for s in range(NSEQ):
                for i in range(NT):
                    wrec(g, ("v_nrm", s, i))
                    ins = g.dma_start(y1d[s][i * 128:(i + 1) * 128, :],
                                      pb(Y1ROT + (i % 4) * C, C))
                    rdma(("y1s", s, i), ins, f"d_y1s{i % 4}")
                if s == 0:
                    for t in range(NBLK):
                        for j in range(NCH):
                            o = 8 * t + j
                            wrec(g, ("v_ev", 0, 16 * t + 8 + j))
                            ins = g.dma_start(
                                qd[0][j * 128:(j + 1) * 128,
                                      t * 512:(t + 1) * 512],
                                pb(QVROT + (o % 4) * 512, 512))
                            rdma(("qv", 0, o), ins, f"d_qv{o % 4}")
                    for sl in range(4):
                        rcmp(("qs_all", sl), f"d_qv{sl}")
                else:
                    for t in range(NBLK):
                        for f in range(24):
                            o = 24 * t + f
                            wrec(g, ("v_ev", 1, o))
                            dst = (qd[1] if f < 8 else
                                   kd1 if f < 16 else vd[1])
                            fc = f % 8
                            ins = g.dma_start(
                                dst[fc * 128:(fc + 1) * 128,
                                    t * 512:(t + 1) * 512],
                                pb(QVROT + (o % 4) * 512, 512))
                            rdma(("qv", 1, o), ins, f"d_qv{o % 4}")
                    for sl in range(4):
                        rcmp(("qv_all", 1, sl), f"d_qv{sl}")

            wrec(g, ("t_mm", 1, 24 * NBLK - 1))
            for w_key, col in (("woT", WO), ("wrT", WR)):
                for cc in range(NCH):
                    inc(g.dma_start(pb(col + cc * 1024, 1024),
                                    w_in[w_key][cc * 128:(cc + 1) * 128, :]),
                        "d_w0", 16)
            for t in range(NBLK):
                for j in range(NCH):
                    o = 32 + 8 * t + j
                    wrec(g, ("v_ev", 0, 64 + 8 * t + j))
                    ins = g.dma_start(
                        vd[0][j * 128:(j + 1) * 128, t * 512:(t + 1) * 512],
                        pb(QVROT + (o % 4) * 512, 512))
                    rdma(("qv", 0, o), ins, f"d_qv{o % 4}")
            wrec(g, ("t_mm", 0, 95))
            for w_key, col in (("wkfT", WKF), ("wvfT", WVF)):
                for cc in range(NCH):
                    inc(g.dma_start(pb(col + cc * 1024, 1024),
                                    w_in[w_key][cc * 128:(cc + 1) * 128, :]),
                        "d_w0", 16)
            rcmp(("w2",), "d_w0")
            for sl in range(4):
                rcmp(("qv_all0", sl), f"d_qv{sl}")

            for s in range(NSEQ):
                for i in range(NT):
                    wrec(g, ("v_x2", s, i))
                    ins = g.dma_start(x2d[s][i * 128:(i + 1) * 128, :],
                                      pf(XLD + (i % 2) * C, C))
                    rdma(("x2s", s, i), ins, f"d_x2s{i % 2}")
                for sl in range(2):
                    rcmp(("x2s_all", s, sl), f"d_x2s{sl}")
                for i in range(NT):
                    wrec(g, ("v_n2", s, i))
                    ins = g.dma_start(y2d[s][1 + i * 128:1 + (i + 1) * 128, :],
                                      pb(Y2ROT + (i % 2) * C, C))
                    rdma(("y2s", s, i), ins, f"d_y2s{i % 2}")
                for sl in range(2):
                    rcmp(("y2s_all", s, sl), f"d_y2s{sl}")
                for e in range(NT):
                    wrec(g, ("v_fin", s, e))
                    ins = g.dma_start(out_d[s, e * 128:(e + 1) * 128, :],
                                      pf(TMP2 + (e % 2) * C, C))
                    rdma(("outst", s, e), ins, f"d_out{e % 2}")
            for sl in range(2):
                wge(g, f"d_out{sl}", tot[f"d_out{sl}"])

        def p1_tensor_entries(s):
            entries = []
            for t in range(NBLK):
                for f in range(24):
                    e = 24 * t + f

                    def emit(tn, t=t, f=f, e=e, s=s):
                        if f == 0:
                            wrec(tn, ("y1f", s, t))
                        if e >= 2:
                            wrec(tn, ("v_ev", s, e - 2))
                        col = (WQ, WK, WV)[f // 8]
                        fc = f % 8
                        for cc in range(NCH):
                            ins = tn.matmul(
                                pAB[:, (f % 2) * 512:(f % 2 + 1) * 512],
                                pb(col + cc * 1024 + fc * 128, 128),
                                pb(Y1F + cc * 512, 512),
                                start=(cc == 0), stop=(cc == NCH - 1))
                        inc(ins, "t_mm", 1)
                        rcmp(("t_mm", s, e), "t_mm")
                    entries.append([emit, 1.8, 0.0, ("eM", s, e)])

                    if 1 <= f <= 16:
                        fq = f - 1
                        gq = 16 * t + fq

                        def emitq(tn, t=t, fq=fq, gq=gq, s=s):
                            wrec(tn, ("v_ev", s, 24 * t + fq))
                            if fq in (0, 8):
                                wrec(tn, ("v_qn_prev", s, t))
                            dst = (pD[0:16, 0:512] if fq < 8
                                   else pD[0:16, 512:1024])
                            ins = tn.matmul(
                                dst, eselq[:, (fq % 8) * 16:(fq % 8 + 1) * 16],
                                pb(SQROT + (fq % 2) * 512, 512),
                                start=(fq % 8 == 0), stop=(fq % 8 == 7))
                            inc(ins, "t_qn", 1)
                            rcmp(("t_qn", s, gq), "t_qn")
                        entries.append([emitq, 0.3, 0.0, ("eQ", s, gq)])
            return entries

        def p1_kq_entries():
            entries = []
            for t in range(NBLK):
                for idx in range(16):
                    f = idx + 8 if idx < 8 else idx - 8
                    e = 16 * t + idx

                    def emit(tn, t=t, f=f, e=e, idx=idx):
                        if idx == 0:
                            wrec(tn, ("y1f", 0, t))
                            if t == 0:
                                wrec(tn, ("w1",))
                        if e >= 2:
                            wrec(tn, ("v_ev", 0, e - 2))
                        col = WK if idx < 8 else WQ
                        fc = f % 8
                        for cc in range(NCH):
                            ins = tn.matmul(
                                pAB[:, (e % 2) * 512:(e % 2 + 1) * 512],
                                pb(col + cc * 1024 + fc * 128, 128),
                                pb(Y1F + cc * 512, 512),
                                start=(cc == 0), stop=(cc == NCH - 1))
                        inc(ins, "t_mm", 1)
                        rcmp(("t_mm", 0, e), "t_mm")
                    entries.append([emit, 1.8, 0.0, ("eM", 0, e)])

                    def emitq(tn, t=t, idx=idx, e=e):
                        wrec(tn, ("v_ev", 0, e))
                        if idx in (0, 8):
                            wrec(tn, ("v_qn_prev", 0, t))
                        dst = (pD[0:16, 512:1024] if idx < 8
                               else pD[0:16, 0:512])
                        ins = tn.matmul(
                            dst, eselq[:, (idx % 8) * 16:(idx % 8 + 1) * 16],
                            pb(SQROT + (e % 2) * 512, 512),
                            start=(idx % 8 == 0), stop=(idx % 8 == 7))
                        inc(ins, "t_qn", 1)
                        rcmp(("t_qn", 0, e), "t_qn")
                    entries.append([emitq, 0.3, 0.0, ("eQ", 0, e)])
            return entries

        def p1_v0_entries():
            entries = []
            for t in range(NBLK):
                for j in range(NCH):
                    e = 64 + 8 * t + j

                    def emitv(tn, t=t, j=j, e=e):
                        if j == 0:
                            wrec(tn, ("y1fv", t))
                        if e == 64:
                            wrec(tn, ("v_ev", 1, 94))
                        elif e == 65:
                            wrec(tn, ("v_ev", 1, 95))
                        else:
                            wrec(tn, ("v_ev", 0, e - 2))
                        for cc in range(NCH):
                            ins = tn.matmul(
                                pAB[:, (e % 2) * 512:(e % 2 + 1) * 512],
                                pb(WV + cc * 1024 + j * 128, 128),
                                pb(Y1F + cc * 512, 512),
                                start=(cc == 0), stop=(cc == NCH - 1))
                        inc(ins, "t_mm", 1)
                        rcmp(("t_mm", 0, e), "t_mm")
                    entries.append([emitv, 1.8, 0.0, ("eM", 0, e)])
            return entries

        def p2_half(tn, s, lt, h, j):
            q = 32 * lt + 2 * h + j
            if h == 0 and j == 0:
                wrec(tn, ("qQ", s, lt))
                if lt == 0:
                    wrec(tn, ("cbias_f32", s))
            if s == 1 and lt == 0 and h == 0 and j == 0:
                wrec(tn, ("Kld",))
            if q >= 2:
                wrec(tn, ("s_ex", s, q - 2))
            elif s == 1:
                wrec(tn, ("s_ex", 0, NHALF - 2 + q))
            po = (h % 2) * 64
            hc = h // 2
            base = (lt % 2) * 1024 + hc * 128
            lhsT = qblk[po:po + 64, base:base + 128]
            for m in range(2):
                mm = 2 * j + m
                rhs = K_sb[po:po + 64, hc * T + mm * 512:hc * T + (mm + 1) * 512]
                ins = tn.matmul(pS[:, (q % 2) * 1024 + m * 512:
                                   (q % 2) * 1024 + (m + 1) * 512],
                                lhsT, rhs, start=True, stop=True)
            inc(ins, "t_sc", 1)
            rcmp(("t_sc_half", s, q), "t_sc")

        def p3_outproj_entries(s):
            entries = []
            for t in range(NBLK):
                def emitT(tn, t=t, s=s):
                    if t == 0:
                        wrec(tn, ("v_qn", 1 - s, NBLK - 1))
                        wrec(tn, ("v_qn", s, NBLK - 1))
                        if s == 1:
                            wrec(tn, ("v_rt", 0, NT - 1))
                    for jj in range(4):
                        lt = 4 * t + jj
                        wrec(tn, ("v_rs", s, lt))
                        if jj == 0:
                            wrec(tn, ("v_rsf_prev", s, t))
                        ins = tn.transpose(
                            pD[0:16, jj * 128:(jj + 1) * 128],
                            rs_f32[:, (s * NT + lt) * NH:(s * NT + lt + 1) * NH],
                            identi[:, :])
                    inc(ins, "t_rsT", 1)
                    rcmp(("t_rsT", s, t), "t_rsT")
                entries.append([emitT, 0.8, 0.0, ("eT", s, t)])

                for cc in range(NCH):
                    def emitB(tn, t=t, cc=cc, s=s):
                        wrec(tn, ("v_rsf", s, t))
                        if cc == 0 and t == 0:
                            wrec(tn, ("vb", s, 0))
                            wrec(tn, ("w2",))
                            wrec(tn, ("esel2",))
                        elif cc == 0:
                            wrec(tn, ("vb", s, t))
                        if 8 * t + cc >= 1:
                            wrec(tn, ("v_vp", s, 8 * t + cc - 1))
                        ins = tn.matmul(pD[:, 512:1024],
                                        pb(ESEL2 + cc * 128, 128, 0, 16),
                                        rsf_sb[0:16, :], start=True, stop=True)
                        inc(ins, "t_B", 1)
                        rcmp(("t_B", s, 8 * t + cc), "t_B")
                    entries.append([emitB, 0.3, 0.0, ("eB", s, t, cc)])

                for jj in range(4):
                    i = 4 * t + jj

                    def emitO(tn, t=t, jj=jj, i=i, s=s):
                        wrec(tn, ("v_vp", s, 8 * t + NCH - 1))
                        if i >= 1:
                            wrec(tn, ("v_x2", s, i - 1))
                        for cc in range(NCH):
                            lhsT = pb(VBLK + cc * 512 + jj * 128, 128)
                            for fh in range(2):
                                ins = tn.matmul(
                                    pAB[:, fh * 512:(fh + 1) * 512], lhsT,
                                    pb(WO + cc * 1024 + fh * 512, 512),
                                    start=(cc == 0), stop=(cc == NCH - 1))
                        inc(ins, "t_op", 1)
                        rcmp(("t_op", s, i), "t_op")
                    entries.append([emitO, 3.6, 0.0, ("eO", s, i)])
            return entries

        def p3_ffn_entries(s):
            entries = []
            for t in range(NBLK):
                for dp in range(NCH):
                    e = 8 * t + dp

                    def emitF(tn, t=t, dp=dp, e=e, s=s):
                        if dp == 0:
                            wrec(tn, ("v_mix", s, 8 * t + NCH - 1))
                        if e >= 2:
                            wrec(tn, ("v_ka", s, e - 2))
                        for cc in range(NCH):
                            ins = tn.matmul(
                                pAB[:, (dp % 2) * 512:(dp % 2 + 1) * 512],
                                pb(WKF + cc * 1024 + dp * 128, 128),
                                pb(XKc + cc * 512, 512),
                                start=(cc == 0), stop=(cc == NCH - 1))
                        inc(ins, "t_kf", 1)
                        rcmp(("t_kf", s, e), "t_kf")
                    entries.append([emitF, 1.8, 0.0, ("eF", s, e)])

                for tt in range(4):
                    e = 4 * t + tt

                    def emitV(tn, t=t, tt=tt, e=e, s=s):
                        if tt == 0:
                            wrec(tn, ("v_ka", s, 8 * t + NCH - 1))
                        if e >= 1:
                            wrec(tn, ("v_rt", s, e - 1))
                        elif s == 1:
                            wrec(tn, ("v_rt", 0, NT - 1))
                        for dp in range(NCH):
                            lhsT = pb(KACT + dp * 512 + tt * 128, 128)
                            for fh in range(2):
                                ins = tn.matmul(
                                    pD[:, fh * 512:(fh + 1) * 512], lhsT,
                                    pb(WVF + dp * 1024 + fh * 512, 512),
                                    start=(dp == 0), stop=(dp == NCH - 1))
                        inc(ins, "t_kv", 1)
                        rcmp(("t_kv", s, e), "t_kv")
                    entries.append([emitV, 3.6, 0.0, ("eV", s, e)])

                    def emitR(tn, t=t, tt=tt, e=e, s=s):
                        wrec(tn, ("v_kt", s, e))
                        for cc in range(NCH):
                            lhsT = pb(Y2F + cc * 512 + tt * 128, 128)
                            for fh in range(2):
                                ins = tn.matmul(
                                    pD[:, fh * 512:(fh + 1) * 512], lhsT,
                                    pb(WR + cc * 1024 + fh * 512, 512),
                                    start=(cc == 0), stop=(cc == NCH - 1))
                        inc(ins, "t_r", 1)
                        rcmp(("t_r", s, e), "t_r")
                    entries.append([emitR, 3.6, 0.0, ("eR", s, e)])
            return entries

        schedule = []
        pos_mark = {}
        sq2shared = [128]
        nrs_early = [2]

        def assemble():
            schedule.clear()
            pos_mark.clear()
            half_idx = [0]

            def pull(queue, budget):
                while (queue and budget >= queue[0][1]
                       and half_idx[0] >= queue[0][2]):
                    fn, cost, _mh, name = queue.pop(0)
                    schedule.append(fn)
                    pos_mark[name] = half_idx[0]
                    if name == ("eO", 0, 15) and queue:
                        queue[0][2] = half_idx[0] + 44
                    if name == ("eR", 0, 15):
                        for q in queue:
                            if q[3] == ("eB", 1, 0, 0):
                                q[2] = half_idx[0] + 16
                                break
                    budget -= cost
                return budget

            for fn, _c, _m, name in p1_kq_entries():
                schedule.append(fn)
                pos_mark[name] = 0
            for s in range(NSEQ):
                if s == 0:
                    partner = p1_tensor_entries(1) + p1_v0_entries()
                    partner[0][2] = 72.0
                else:
                    partner = (p3_outproj_entries(0) + p3_ffn_entries(0) +
                               p3_outproj_entries(1))
                    partner[0][2] = 512 + 8.0
                debt = 0.0
                for lt in range(NT):
                    for h in range(NH):
                        for j in range(2):
                            def mk(s=s, lt=lt, h=h, j=j):
                                return lambda tn: p2_half(tn, s, lt, h, j)
                            schedule.append(mk())
                            half_idx[0] = 32 * lt + 2 * h + j + s * NHALF
                            debt = pull(partner, min(debt + 0.62, 6.0))
                while partner:
                    fn, _c, _m, name = partner.pop(0)
                    half_idx[0] += _c / 0.62
                    schedule.append(fn)
                    pos_mark[name] = half_idx[0]
            for fn, _c, _m, name in p3_ffn_entries(1):
                half_idx[0] += _c / 0.62
                schedule.append(fn)
                pos_mark[name] = half_idx[0]

        def f_tensor(tn):
            for fn in schedule:
                fn(tn)

        def f_vector(v):
            wge(v, "g_io", 1)
            wrec(v, ("wh",))
            v.tensor_single_scalar(identi[:, :], identi[:, :], 0.0,
                                   ALU.is_equal)
            v.drain()
            v.tensor_single_scalar(pg[:, :], pi[:, :], 63.5, ALU.is_gt)
            v.drain()
            for cc in range(NCH):
                v.tensor_scalar(eqt[:, :], eji[:, :], float(-2 * cc), None,
                                ALU.add)
                v.drain()
                v.tensor_scalar(eselq[:, cc * 16:(cc + 1) * 16], eqt[:, :],
                                pg[:, 0:1], None, ALU.is_equal)
                v.drain()
            v.memset(epsv[:, :], EPS)
            v.drain()
            v.memset(zrow, 0.0)
            dinc(v, "v_set")

            def ln_stats(s):
                for i in range(NT):
                    wrec(v, ("xA", s, i))
                    xt = pf(XROT + (i % 2) * C, C)
                    v.reduce_sum(sums[:, i:i + 1], xt, axis=AXX)
                    v.scalar_tensor_tensor(sdump[:, :], xt, 1.0, xt,
                                           ALU.bypass, ALU.mult,
                                           accum_out=ssqs[:, i:i + 1])
                    dinc(v, "v_st")
                    rcmp(("v_st", s, i), "v_st")

            def ln_batch(key):
                v.tensor_scalar(meanv[:, :], sums[:, :], 1.0 / C, None,
                                ALU.mult)
                v.tensor_scalar(varv[:, :], ssqs[:, :], 1.0 / C, None,
                                ALU.mult)
                v.drain()
                v.tensor_mul(m2t[:, :], meanv[:, :], meanv[:, :])
                v.drain()
                v.tensor_sub(varv[:, :], varv[:, :], m2t[:, :])
                dinc(v, "v_ti")
                rcmp(key, "v_ti")

            def ln_inv(skey):
                wrec(v, skey)
                v.reciprocal(invv[:, :], stdv[:, :])
                v.drain()
                v.scalar_tensor_tensor(negmi[:, :], meanv[:, :], -1.0,
                                       invv[:, :], ALU.mult, ALU.mult)
                dinc(v, "v_ti")

            def p1_norm(s):
                for i in range(NT):
                    wrec(v, ("xB", s, i))
                    if s == 0 and i == 0:
                        wrec(v, ("zst",))
                    if i >= 4:
                        wrec(v, ("y1s", s, i - 4))
                    elif s == 1:
                        wrec(v, ("y1s", 0, NT - 4 + i))
                    v.tensor_scalar(pb(Y1ROT + (i % 4) * C, C),
                                    pf(XROT + (i % 2) * C, C),
                                    invv[:, i:i + 1], negmi[:, i:i + 1],
                                    ALU.mult, ALU.add)
                    dinc(v, "v_nrm")
                    rcmp(("v_nrm", s, i), "v_nrm")

            def cbias_build(s):
                v.reduce_max(kmax2[0:16, 0:1], qkn_sb[32:48, :], axis=AXX)
                v.drain()
                v.tensor_scalar(negk16[0:16, 0:1], kmax2[0:16, 0:1],
                                -1.0 / 16.0, None, ALU.mult)
                v.drain()
                v.tensor_scalar(qkn_sb[0:16, :], qkn_sb[0:16, :], -1.0 / 16.0,
                                negk16[0:16, 0:1], ALU.mult, ALU.add)
                dinc(v, "v_cb")
                rcmp(("v_cb", s), "v_cb")
                wrec(v, ("cbT", s))
                v.tensor_copy(cbias[:, s * 256:(s + 1) * 256],
                              cbias_bf[:, s * 256:(s + 1) * 256])
                dinc(v, "v_cb")
                rcmp(("cbias_f32", s), "v_cb")

            def p1_evacs_kq():
                rec[("v_qn_prev", 0, 0)] = ("v_qn", tot["v_qn"])
                for t in range(NBLK):
                    for idx in range(16):
                        e = 16 * t + idx
                        f = idx + 8 if idx < 8 else idx - 8
                        wrec(v, ("t_mm", 0, e))
                        bias = bvec["bkc"] if idx < 8 else bvec["bqc"]
                        fc = f % 8
                        src = pAB[:, (e % 2) * 512:(e % 2 + 1) * 512]
                        if idx < 8:
                            dstt = K_sb[:, fc * T + t * 512:
                                        fc * T + (t + 1) * 512]
                        else:
                            o = 8 * t + idx - 8
                            if o >= 4:
                                wrec(v, ("qv", 0, o - 4))
                            dstt = pb(QVROT + (o % 4) * 512, 512)
                        v.tensor_scalar(dstt, src, bias[:, fc:fc + 1], None,
                                        ALU.add)
                        if e >= 2:
                            wrec(v, ("t_qn", 0, e - 2))
                        v.drain()
                        v.tensor_mul(pb(SQROT + (e % 2) * 512, 512),
                                     dstt, dstt)
                        dinc(v, "v_ev")
                        rcmp(("v_ev", 0, e), "v_ev")
                    wrec(v, ("t_qn", 0, 16 * t + 7))
                    v.tensor_copy(qkn_sb[32:48, t * 512:(t + 1) * 512],
                                  pD[0:16, 512:1024])
                    v.drain()
                    wrec(v, ("t_qn", 0, 16 * t + 15))
                    v.tensor_copy(qkn_sb[0:16, t * 512:(t + 1) * 512],
                                  pD[0:16, 0:512])
                    dinc(v, "v_qn")
                    rcmp(("v_qn", 0, t), "v_qn")
                    rec[("v_qn_prev", 0, t + 1)] = ("v_qn", tot["v_qn"])
                cbias_build(0)

            def p1_evacs_v0():
                for t in range(NBLK):
                    for j in range(NCH):
                        e = 64 + 8 * t + j
                        o = 32 + 8 * t + j
                        wrec(v, ("t_mm", 0, e))
                        if o - 32 < 4:
                            wrec(v, ("qv", 1, 92 + o - 32))
                        else:
                            wrec(v, ("qv", 0, o - 4))
                        v.tensor_scalar(pb(QVROT + (o % 4) * 512, 512),
                                        pAB[:, (e % 2) * 512:
                                            (e % 2 + 1) * 512],
                                        bvec["bvc"][:, j:j + 1], None, ALU.add)
                        dinc(v, "v_ev")
                        rcmp(("v_ev", 0, e), "v_ev")

            def p1_evacs(s):
                rec[("v_qn_prev", s, 0)] = ("v_qn", tot["v_qn"])
                for t in range(NBLK):
                    for f in range(24):
                        e = 24 * t + f
                        wrec(v, ("t_mm", s, e))
                        bias = (bvec["bqc"] if f < 8 else
                                bvec["bkc"] if f < 16 else bvec["bvc"])
                        fc = f % 8
                        src = pAB[:, (f % 2) * 512:(f % 2 + 1) * 512]
                        o = 24 * t + f
                        if o >= 4:
                            wrec(v, ("qv", s, o - 4))
                        else:
                            wrec(v, ("qv", 0, 28 + o))
                        dstt = pb(QVROT + (o % 4) * 512, 512)
                        v.tensor_scalar(dstt, src, bias[:, fc:fc + 1], None,
                                        ALU.add)
                        if f < 16:
                            gq = 16 * t + f
                            if gq >= 2:
                                wrec(v, ("t_qn", s, gq - 2))
                            v.drain()
                            v.tensor_mul(pb(SQROT + (f % 2) * 512, 512),
                                         dstt, dstt)
                        dinc(v, "v_ev")
                        rcmp(("v_ev", s, e), "v_ev")
                    wrec(v, ("t_qn", s, 16 * t + 7))
                    v.tensor_copy(qkn_sb[0:16, t * 512:(t + 1) * 512],
                                  pD[0:16, 0:512])
                    v.drain()
                    wrec(v, ("t_qn", s, 16 * t + 15))
                    v.tensor_copy(qkn_sb[32:48, t * 512:(t + 1) * 512],
                                  pD[0:16, 512:1024])
                    dinc(v, "v_qn")
                    rcmp(("v_qn", s, t), "v_qn")
                    rec[("v_qn_prev", s, t + 1)] = ("v_qn", tot["v_qn"])

            def p2_rs(s, lt):
                wrec(v, ("s_exd", s, lt))
                base = s * NHALF + 32 * lt
                v.tensor_add(rs16[:, :], denom[:, base:base + 16],
                             denom[:, base + 16:base + 32])
                v.drain()
                v.reciprocal(rcp16[:, :], rs16[:, :])
                v.drain()
                v.tensor_mul(rs_f32[:, (s * NT + lt) * NH:
                                    (s * NT + lt + 1) * NH],
                             rs16[:, :], rcp16[:, :])
                dinc(v, "v_rs")
                rcmp(("v_rs", s, lt), "v_rs")

            def p3_partA(s, rs_cb=None):
                if s == 0:
                    for cc in range(NCH):
                        v.tensor_single_scalar(pb(ESEL2 + cc * 128, 128, 0, 16),
                                               es2i[:, :], float(-2 * cc),
                                               ALU.is_equal)
                        v.drain()
                    dinc(v, "v_set")
                    rcmp(("esel2",), "v_set")
                rec[("v_rsf_prev", s, 0)] = ("v_rsf", tot["v_rsf"])
                for t in range(NBLK):
                    if rs_cb is not None:
                        rs_cb(t)
                    wrec(v, ("t_rsT", s, t))
                    if t >= 1:
                        wrec(v, ("t_B", s, 8 * (t - 1) + 7))
                    elif s == 1:
                        wrec(v, ("t_B", 0, 8 * NBLK - 1))
                    v.tensor_copy(rsf_sb[0:16, :], pD[0:16, 0:512])
                    dinc(v, "v_rsf")
                    rcmp(("v_rsf", s, t), "v_rsf")
                    rec[("v_rsf_prev", s, t + 1)] = ("v_rsf", tot["v_rsf"])
                    for cc in range(NCH):
                        wrec(v, ("t_B", s, 8 * t + cc))
                        sl = pb(VBLK + cc * 512, 512)
                        v.tensor_mul(sl, sl, pD[:, 512:1024])
                        dinc(v, "v_vp")
                        rcmp(("v_vp", s, 8 * t + cc), "v_vp")
                    for jj in range(4):
                        i = 4 * t + jj
                        wrec(v, ("t_op", s, i))
                        wrec(v, ("xld", s, i))
                        if s == 0 and i == 0:
                            wrec(v, ("wh",))
                        xt = pf(XLD + (i % 2) * C, C)
                        v.tensor_add(xt, xt, bob[:, :])
                        v.drain()
                        v.tensor_add(xt, xt, pAB[:, :])
                        v.drain()
                        v.reduce_sum(sums[:, i:i + 1], xt, axis=AXX)
                        v.scalar_tensor_tensor(sdump[:, :], xt, 1.0, xt,
                                               ALU.bypass, ALU.mult,
                                               accum_out=ssqs[:, i:i + 1])
                        dinc(v, "v_x2")
                        rcmp(("v_x2", s, i), "v_x2")

            def fin_one(s, e):
                wrec(v, ("s_th", s, e))
                wrec(v, ("x2f", s, e))
                if e >= 2:
                    wrec(v, ("outst", s, e - 2))
                elif s == 1:
                    wrec(v, ("outst", 0, NT - 2 + e))
                ts_ = pf(RT + (e % 2) * C, C)
                os_ = pf(TMP2 + (e % 2) * C, C)
                v.tensor_scalar(ts_, ts_, 0.5, 0.5, ALU.mult, ALU.add)
                v.drain()
                v.tensor_mul(os_, ts_, pf(KTMP + (e % 2) * C, C))
                v.drain()
                v.tensor_add(os_, os_, pf(X2RD + (e % 2) * C, C))
                dinc(v, "v_fin")
                rcmp(("v_fin", s, e), "v_fin")

            def p3_partB(s):
                ln_batch(("v_ti_ln2", s))
                ln_inv(("s_sq", s, "ln2"))
                for i in range(NT):
                    wrec(v, ("x2l", s, i))
                    if i >= 2:
                        wrec(v, ("y2s", s, i - 2))
                    v.tensor_scalar(pb(Y2ROT + (i % 2) * C, C),
                                    pf(X2RD + (i % 2) * C, C),
                                    invv[:, i:i + 1], negmi[:, i:i + 1],
                                    ALU.mult, ALU.add)
                    dinc(v, "v_n2")
                    rcmp(("v_n2", s, i), "v_n2")
                for t in range(NBLK):
                    for cc in range(NCH):
                        wrec(v, ("y2f", s, t))
                        wrec(v, ("ypv", s, t, cc))
                        if t >= 1 and cc == 0:
                            wrec(v, ("t_kf", s, 8 * (t - 1) + NCH - 1))
                        yf = pb(Y2F + cc * 512, 512)
                        yp = pb(YPV + (cc % 2) * 512, 512)
                        v.tensor_sub(yf, yf, yp)
                        v.drain()
                        v.scalar_tensor_tensor(pb(XKc + cc * 512, 512), yf,
                                               bvec["mukc"][:, cc:cc + 1],
                                               yp, ALU.mult, ALU.add)
                        v.drain()
                        v.scalar_tensor_tensor(yf, yf,
                                               bvec["murc"][:, cc:cc + 1],
                                               yp, ALU.mult, ALU.add)
                        dinc(v, "v_mix")
                        rcmp(("v_mix", s, 8 * t + cc), "v_mix")
                    for dp in range(NCH):
                        e = 8 * t + dp
                        wrec(v, ("t_kf", s, e))
                        if t >= 1 and dp == 0:
                            wrec(v, ("t_kv", s, 4 * (t - 1) + 3))
                        src = pAB[:, (dp % 2) * 512:(dp % 2 + 1) * 512]
                        v.tensor_scalar(sdump[:, 0:512], src, 0.0, None,
                                        ALU.max)
                        v.drain()
                        v.tensor_mul(pb(KACT + dp * 512, 512),
                                     sdump[:, 0:512], sdump[:, 0:512])
                        dinc(v, "v_ka")
                        rcmp(("v_ka", s, e), "v_ka")
                    for tt in range(4):
                        e = 4 * t + tt
                        wrec(v, ("t_kv", s, e))
                        v.tensor_copy(pf(KTMP + (e % 2) * C, C), pD[:, :])
                        dinc(v, "v_kt")
                        rcmp(("v_kt", s, e), "v_kt")
                        wrec(v, ("t_r", s, e))
                        if e >= 2:
                            wrec(v, ("s_th", s, e - 2))
                        v.tensor_copy(pf(RT + (e % 2) * C, C), pD[:, :])
                        dinc(v, "v_rt")
                        rcmp(("v_rt", s, e), "v_rt")
                        if e >= 1:
                            fin_one(s, e - 1)
                if True:
                    fin_one(s, NT - 1)

            ln_stats(0)
            ln_batch(("v_ti_ln1", 0))
            ln_inv(("s_sq", 0, "ln1"))
            p1_norm(0)
            p1_evacs_kq()
            ln_stats(1)
            ln_batch(("v_ti_ln1", 1))
            ln_inv(("s_sq", 1, "ln1"))
            p1_norm(1)
            p1_evacs(1)
            p1_evacs_v0()
            cbias_build(1)
            for lt in range(NT):
                p2_rs(0, lt)
            dinc(v, "v_set")
            rcmp(("p1_done_v", 1), "v_set")
            p3_partA(0)
            for lt in range(nrs_early[0]):
                p2_rs(1, lt)
            p3_partB(0)
            nxt = [nrs_early[0]]

            def rs_cb(t):
                while nxt[0] <= 4 * t + 3:
                    p2_rs(1, nxt[0])
                    nxt[0] += 1
            p3_partA(1, rs_cb)
            while nxt[0] < NT:
                p2_rs(1, nxt[0])
                nxt[0] += 1
            p3_partB(1)

        def f_scalar(sc):
            def sqrtln(s, which):
                wrec(sc, (("v_ti_ln1", s) if which == "ln1"
                          else ("v_ti_ln2", s)))
                sc.activation(stdv[:, :], varv[:, :], AF.Sqrt,
                              bias=epsv[:, 0:1])
                dinc(sc, "s_sq")
                rcmp(("s_sq", s, which), "s_sq")

            def tanh_one(s, e):
                wrec(sc, ("v_rt", s, e))
                sc.activation(pf(RT + (e % 2) * C, C), pf(RT + (e % 2) * C, C),
                              AF.Tanh, scale=0.5)
                dinc(sc, "s_th")
                rcmp(("s_th", s, e), "s_th")

            def exps(s, inserts):
                for lt in range(NT):
                    for h in range(NH):
                        for j in range(2):
                            q = 32 * lt + 2 * h + j
                            for fn in inserts.pop(q, []):
                                fn()
                            wrec(sc, ("t_sc_half", s, q))
                            ins = sc.activation(
                                dump8[:, (q % 4) * 1024:(q % 4 + 1) * 1024],
                                pS[:, (q % 2) * 1024:(q % 2 + 1) * 1024],
                                AF.Exp,
                                bias=cbias[:, (s * NT + lt) * NH + h:
                                           (s * NT + lt) * NH + h + 1],
                                scale=SCALE,
                                accum_out=denom[:, s * NHALF + 32 * lt +
                                                16 * j + h:
                                                s * NHALF + 32 * lt +
                                                16 * j + h + 1])
                            inc(ins, "s_ex", 1)
                            rcmp(("s_ex", s, q), "s_ex")
                            if q % 4 == 3:
                                sc.drain()
                    sc.drain(fusable=True).then_inc(sems["s_exd"], 1)
                    tot["s_exd"] += 1
                    rcmp(("s_exd", s, lt), "s_exd")

            sqrtln(0, "ln1")
            exps(0, {36: [lambda: sqrtln(1, "ln1")]})
            ins1 = {}
            sq2pos = sq2shared[0]
            ins1.setdefault(sq2pos, []).append(lambda: sqrtln(0, "ln2"))
            done_tanh = [0]
            prevq = sq2pos
            for e in range(NT):
                p = pos_mark[("eR", 0, e)]
                qpos = min(max(p - 512 + 8, prevq + 2), 508)
                prevq = qpos
                ins1.setdefault(qpos, []).append(
                    (lambda e=e: (tanh_one(0, e),
                                  done_tanh.__setitem__(0, e + 1))))
            exps(1, ins1)
            while done_tanh[0] < NT:
                tanh_one(0, done_tanh[0])
                done_tanh[0] += 1
            sqrtln(1, "ln2")
            for e in range(NT):
                tanh_one(1, e)

        assemble()
        sq2shared[0] = max(min(pos_mark[("eO", 0, 15)] - 512 + 8, 440), 128)
        nrs_early[0] = max(0, min(NT, (sq2shared[0] - 32) // 32 + 1))
        sections = (f_sync, f_gpsimd, f_vector, f_scalar, f_tensor)
        stub = _Stub()
        for f in sections:
            f(stub)
        for nm in tot:
            tot[nm] = 0
        dry[0] = False

        blk = st.enter_context(nc.Block())

        @blk.sync
        def _(sync):
            f_sync(sync)

        @blk.gpsimd
        def _(g):
            f_gpsimd(g)

        @blk.vector
        def _(v):
            f_vector(v)

        @blk.scalar
        def _(sc):
            f_scalar(sc)

        @blk.tensor
        def _(tn):
            f_tensor(tn)

    return nc


def _prep_inputs(L0_attn_x, L0_attn_w_in, L0_attn_b_in, L0_attn_w_out,
                 L0_attn_b_out, L0_ffn_W_r, L0_ffn_W_k, L0_ffn_W_v,
                 L0_ffn_mu_r, L0_ffn_mu_k):
    x = np.ascontiguousarray(np.asarray(L0_attn_x, dtype=np.float32))
    w3 = np.asarray(L0_attn_w_in, dtype=np.float32).reshape(NH, 3, DH, C)
    wq = w3[:, 0].reshape(C, C)
    wk = w3[:, 1].reshape(C, C)
    wv = w3[:, 2].reshape(C, C)
    b3 = np.asarray(L0_attn_b_in, dtype=np.float32).reshape(NH, 3, DH)
    bq, bk, bv = (b3[:, i].reshape(C) for i in range(3))

    def colv(v):
        return np.ascontiguousarray(np.asarray(v, np.float32).reshape(8, 128).T)

    common = {
        "wqT": np.ascontiguousarray(wq.T),
        "wkT": np.ascontiguousarray(wk.T),
        "wvT": np.ascontiguousarray(wv.T),
        "woT": np.ascontiguousarray(np.asarray(L0_attn_w_out, np.float32).T),
        "wrT": np.ascontiguousarray(np.asarray(L0_ffn_W_r, np.float32).T),
        "wkfT": np.ascontiguousarray(np.asarray(L0_ffn_W_k, np.float32).T),
        "wvfT": np.ascontiguousarray(np.asarray(L0_ffn_W_v, np.float32).T),
        "bqc": colv(bq), "bkc": colv(bk), "bvc": colv(bv),
        "murc": colv(L0_ffn_mu_r), "mukc": colv(L0_ffn_mu_k),
        "bob": np.ascontiguousarray(
            np.broadcast_to(np.asarray(L0_attn_b_out, np.float32), (128, C))),
    }
    in_maps = []
    for core in range(8):
        m = dict(common)
        m["x"] = np.ascontiguousarray(x[2 * core:2 * core + 2])
        in_maps.append(m)
    return in_maps


def kernel(**inputs):
    in_maps = _prep_inputs(**inputs)
    nc = build()
    res = run_bass_kernel_spmd(nc, in_maps, core_ids=list(range(8)))
    return np.concatenate([r["out"] for r in res.results], axis=0)


if __name__ == "__main__":
    nc = build()
    print("build ok")


# version 07 (final)
# speedup vs baseline: 1.0547x; 1.0547x over previous
import contextlib
import numpy as np

import concourse.bass as bass
import concourse.mybir as mybir
from concourse.bass_utils import run_bass_kernel_spmd

F32 = mybir.dt.float32
BF16 = mybir.dt.bfloat16
AF = mybir.ActivationFunctionType
ALU = mybir.AluOpType
AXX = mybir.AxisListType.X

T = 2048
C = 1024
NH = 16
DH = 64
NSEQ = 2
NT = 16
NBLK = 4
NCH = 8
NHALF = 2 * NH * NT
EPS = 1e-5
SCALE = 0.125

DMA_GROUPS = {
    "d_x": 3, "d_y1l": 2, "d_qQ": 2, "d_vb": 1, "d_x2l": 2, "d_y2f": 1,
    "d_ypv": 2, "d_cbT": 1, "d_wh": 1, "d_kld": 1,
    "d_w": 2, "d_y1s": 4, "d_qv": 4, "d_x2s": 3, "d_y2s": 2, "d_out": 2,
    "d_z": 1,
}
CMP_SEMS = ("v_st", "v_ti", "v_nrm", "v_ev", "v_qn", "v_cb", "v_rs", "v_rsf",
            "v_vp", "v_x2", "v_mix", "v_ka", "v_kt", "v_rt", "v_fin", "v_n2",
            "v_set", "t_mm", "t_qn", "t_sc", "t_B", "t_op", "t_kf", "t_kv",
            "t_r", "t_rsT", "s_sq", "s_ex", "s_exd", "s_th", "g_io")


def dma_names():
    out = []
    for base, n in DMA_GROUPS.items():
        out += [f"{base}{i}" for i in range(n)]
    return out


class _StubIns:
    def then_inc(self, *a, **k):
        return self


class _Stub:
    def __getattr__(self, k):
        return lambda *a, **kw: _StubIns()


def build(debug=False):
    nc = bass.Bass("TRN2", target_bir_lowering=False)

    x_d = nc.dram_tensor("x", [NSEQ, T, C], F32, kind="ExternalInput").ap()
    w_in = {}
    for nm in ("wqT", "wkT", "wvT", "woT", "wrT", "wkfT", "wvfT"):
        w_in[nm] = nc.dram_tensor(nm, [C, C], F32, kind="ExternalInput").ap()
    vecs = {}
    for nm in ("bqc", "bkc", "bvc", "murc", "mukc"):
        vecs[nm] = nc.dram_tensor(nm, [128, 8], F32, kind="ExternalInput").ap()
    bob_d = nc.dram_tensor("bob", [128, C], F32, kind="ExternalInput").ap()
    out_d = nc.dram_tensor("out", [NSEQ, T, C], F32, kind="ExternalOutput").ap()

    kind_s = "ExternalOutput" if debug else "Internal"
    y1d = [nc.dram_tensor(f"y1d{s}", [T, C], BF16, kind=kind_s).ap() for s in range(NSEQ)]
    qd = [nc.dram_tensor(f"qd{s}", [C, T], BF16, kind=kind_s).ap() for s in range(NSEQ)]
    vd = [nc.dram_tensor(f"vd{s}", [C, T], BF16, kind=kind_s).ap() for s in range(NSEQ)]
    kd1 = nc.dram_tensor("kd1", [C, T], BF16, kind=kind_s).ap()
    x2d = [nc.dram_tensor(f"x2d{s}", [T, C], F32, kind=kind_s).ap() for s in range(NSEQ)]
    y2d = [nc.dram_tensor(f"y2d{s}", [T + 1, C], BF16, kind=kind_s).ap() for s in range(NSEQ)]

    tot = {}
    rec = {}
    sems = {}
    dry = [True]

    st = contextlib.ExitStack()
    with st:
        for nm in dma_names() + list(CMP_SEMS):
            sems[nm] = st.enter_context(nc.semaphore(nm))
            tot[nm] = 0

        def sb(name, shape, dt):
            return st.enter_context(nc.sbuf_tensor(name, shape, dt))

        def inc(ins, name, k=1):
            tot[name] += k
            ins.then_inc(sems[name], k)
            return tot[name]

        def dinc(eng, name, k=1):
            tot[name] += k
            eng.drain(fusable=True).then_inc(sems[name], k)
            return tot[name]

        def wge(eng, name, val):
            if val > 0:
                eng.wait_ge(sems[name], val)

        def wrec(eng, key):
            if key not in rec:
                assert dry[0], f"missing rec {key}"
                return
            nm, v = rec[key]
            wge(eng, nm, v)

        def rdma(key, ins, name):
            rec[key] = (name, inc(ins, name, 16))

        def rcmp(key, name):
            rec[key] = (name, tot[name])

        K_sb = sb("K_sb", [128, NCH * T], BF16)
        qblk = sb("qblk", [128, 2 * 1024], BF16)
        bob = sb("bob_sb", [128, C], F32)
        dump8 = sb("dump8", [128, 4 * 1024], mybir.dt.float8e4)
        sdump = sb("sdump", [128, 1024], BF16)
        cbias = sb("cbias", [128, NSEQ * NT * NH], F32)
        cbias_bf = sb("cbias_bf", [128, NSEQ * NT * NH], BF16)
        denom = sb("denom", [128, NSEQ * NHALF], F32)
        rs16 = sb("rs16", [128, NH], F32)
        rcp16 = sb("rcp16", [128, NH], F32)
        rs_f32 = sb("rs_f32", [128, NSEQ * NT * NH], F32)
        kmax2 = sb("kmax2", [16, 1], F32)
        negk16 = sb("negk16", [16, 1], F32)
        bvec = {nm: sb(nm + "_sb", [128, 8], F32) for nm in
                ("bqc", "bkc", "bvc", "murc", "mukc")}
        identi = sb("identi", [128, 128], F32)
        es2i = sb("es2i", [16, 128], F32)
        eji = sb("eji", [128, 16], F32)
        pg = sb("pg", [128, 1], F32)
        pi = sb("pi", [128, 1], F32)
        eqt = sb("eqt", [128, 16], F32)
        eselq = sb("eselq", [128, NCH * 16], BF16)
        epsv = sb("epsv", [128, 1], F32)
        sums = sb("sums", [128, NT], F32)
        ssqs = sb("ssqs", [128, NT], F32)
        m2t = sb("m2t", [128, NT], F32)
        meanv = sb("meanv", [128, NT], F32)
        varv = sb("varv", [128, NT], F32)
        stdv = sb("stdv", [128, NT], F32)
        invv = sb("invv", [128, NT], F32)
        negmi = sb("negmi", [128, NT], F32)
        rsf_sb = sb("rsf_sb", [16, 512], BF16)
        qkn_sb = sb("qkn_sb", [48, T], BF16)

        PB = 53248
        poolB = sb("poolB", [128, PB], BF16)
        WQ, WK, WV = 0, 8192, 16384
        Y1ROT, Y1F, QVROT, SQROT = 24576, 28672, 32768, 34816
        WO, WR, WKF, WVF = 0, 8192, 16384, 24576
        VBLK, Y2F, XKc, KACT, Y2ROT, YPV, ESEL2 = (32768, 36864, 40960, 45056,
                                                   49152, 51200, 52224)
        zrow = poolB[0:1, Y1ROT:Y1ROT + C]

        PF = 10240
        poolF = sb("poolF", [128, PF], F32)
        XROT = 0
        XLD, X2RD, KTMP, TMP2, RT = 0, 2048, 4096, 6144, 8192

        def pb(col, n, p0=0, p1=128):
            return poolB[p0:p1, col:col + n]

        def pf(col, n):
            return poolF[:, col:col + n]

        pS = st.enter_context(nc.psum_tensor("pS", [128, 2048], F32))
        pAB = st.enter_context(nc.psum_tensor("pAB", [128, 1024], F32))
        pD = st.enter_context(nc.psum_tensor("pD", [128, 1024], F32))

        def xa_load(sync, s, i):
            if s == 1 and i < 2:
                wrec(sync, ("v_nrm", 0, NT - 2 + i))
            if i >= 2:
                wrec(sync, ("v_st", s, i - 2))
            rdma(("xA", s, i),
                 sync.dma_start(pf(XROT + (i % 2) * C, C),
                                x_d[s, i * 128:(i + 1) * 128, :]),
                 f"d_x{i % 2}")

        def xb_load(sync, s, i):
            if i < 2:
                wrec(sync, ("v_st", s, NT - 1))
            else:
                wrec(sync, ("v_nrm", s, i - 2))
            rdma(("xB", s, i),
                 sync.dma_start(pf(XROT + (i % 2) * C, C),
                                x_d[s, i * 128:(i + 1) * 128, :]),
                 f"d_x{i % 2}")

        def y1f_load(sync, s, t):
            for j in range(4):
                wrec(sync, ("y1s", s, 4 * t + j))
            if s == 0:
                if t >= 1:
                    wrec(sync, ("t_mm", 0, 16 * t - 1))
            else:
                if t == 0:
                    wrec(sync, ("t_mm", 0, 63))
                else:
                    wrec(sync, ("t_mm", 1, 24 * t - 1))
            for cc in range(NCH):
                ins = sync.dma_start(
                    pb(Y1F + cc * 512, 512),
                    y1d[s][t * 512:(t + 1) * 512, cc * 128:(cc + 1) * 128],
                    transpose=True)
                inc(ins, "d_y1l0", 16)
            rcmp(("y1f", s, t), "d_y1l0")

        def y1fv_load(sync, t):
            if t == 0:
                wrec(sync, ("t_mm", 1, 24 * NBLK - 1))
            else:
                wrec(sync, ("t_mm", 0, 63 + 8 * t))
            for cc in range(NCH):
                ins = sync.dma_start(
                    pb(Y1F + cc * 512, 512),
                    y1d[0][t * 512:(t + 1) * 512, cc * 128:(cc + 1) * 128],
                    transpose=True)
                inc(ins, "d_y1l0", 16)
            rcmp(("y1fv", t), "d_y1l0")

        def cbt_load(sync, s):
            wrec(sync, ("v_cb", s))
            for lt in range(NT):
                ins = sync.dma_start(
                    cbias_bf[:, (s * NT + lt) * NH:(s * NT + lt + 1) * NH],
                    qkn_sb[0:16, lt * 128:(lt + 1) * 128], transpose=True)
                inc(ins, "d_cbT0", 16)
            rcmp(("cbT", s), "d_cbT0")

        def qq_load(sync, s, lt):
            if lt == 0 and s == 0:
                for sl in range(4):
                    wrec(sync, ("qs_all", sl))
            elif lt == 0:
                for sl in range(4):
                    wrec(sync, ("qv_all", s, sl))
            if lt >= 2:
                wrec(sync, ("t_sc_half", s, 32 * (lt - 1) - 1))
            for cc in range(NCH):
                ins = sync.dma_start(
                    qblk[:, (lt % 2) * 1024 + cc * 128:
                         (lt % 2) * 1024 + (cc + 1) * 128],
                    qd[s][cc * 128:(cc + 1) * 128, lt * 128:(lt + 1) * 128])
                inc(ins, f"d_qQ{lt % 2}", 16)
            rcmp(("qQ", s, lt), f"d_qQ{lt % 2}")

        def kld_load(sync, fc):
            wrec(sync, ("qv", 1, 80 + fc))
            wrec(sync, ("t_sc_half", 0, 483 + 4 * fc))
            ins = sync.dma_start(K_sb[:, fc * T:(fc + 1) * T],
                                 kd1[fc * 128:(fc + 1) * 128, :])
            rec[("Kld",)] = ("d_kld0", inc(ins, "d_kld0", 16))

        def vblk_load(sync, s, t):
            if s == 0 and t == 0:
                wrec(sync, ("p1_done_v", 1))
                wrec(sync, ("t_mm", 1, 24 * NBLK - 1))
                wrec(sync, ("t_qn", 1, 16 * NBLK - 1))
                for sl in range(4):
                    wrec(sync, ("qv_all0", sl))
            if t >= 1:
                wrec(sync, ("t_op", s, 4 * t - 1))
            elif s == 1:
                wrec(sync, ("t_op", 0, 4 * NBLK - 1))
            for cc in range(NCH):
                ins = sync.dma_start(
                    pb(VBLK + cc * 512, 512),
                    vd[s][cc * 128:(cc + 1) * 128, t * 512:(t + 1) * 512])
                inc(ins, "d_vb0", 16)
            rcmp(("vb", s, t), "d_vb0")

        def xld_load(sync, s, i):
            if i >= 2:
                wrec(sync, ("x2s", s, i - 2))
            elif s == 1:
                wrec(sync, ("v_x2", 0, NT - 1))
            rdma(("xld", s, i),
                 sync.dma_start(pf(XLD + (i % 2) * C, C),
                                x_d[s, i * 128:(i + 1) * 128, :]),
                 f"d_x{i % 2}")

        def x2l_load(sync, s, i):
            wrec(sync, ("x2s", s, i))
            if i >= 2:
                wrec(sync, ("v_n2", s, i - 2))
            elif s == 1:
                wrec(sync, ("v_fin", 0, NT - 2 + i))
            rdma(("x2l", s, i),
                 sync.dma_start(pf(X2RD + (i % 2) * C, C),
                                x2d[s][i * 128:(i + 1) * 128, :]),
                 f"d_x2l{i % 2}")

        def y2f_load(sync, s, t):
            wrec(sync, ("y2s", s, 4 * t + 2))
            wrec(sync, ("y2s", s, 4 * t + 3))
            if t == 0:
                if s == 1:
                    wrec(sync, ("t_r", 0, 4 * NBLK - 1))
            else:
                wrec(sync, ("t_r", s, 4 * t - 1))
            for cc in range(NCH):
                ins = sync.dma_start(
                    pb(Y2F + cc * 512, 512),
                    y2d[s][1 + t * 512:1 + (t + 1) * 512,
                           cc * 128:(cc + 1) * 128],
                    transpose=True)
                inc(ins, "d_y2f0", 16)
            rcmp(("y2f", s, t), "d_y2f0")

        def ypv_load(sync, s, t, cc):
            if 8 * t + cc >= 2:
                wrec(sync, ("v_mix", s, 8 * t + cc - 2))
            elif s == 1:
                wrec(sync, ("v_mix", 0, 8 * NBLK - 2 + cc))
            ins = sync.dma_start(
                pb(YPV + (cc % 2) * 512, 512),
                y2d[s][t * 512:(t + 1) * 512, cc * 128:(cc + 1) * 128],
                transpose=True)
            rdma(("ypv", s, t, cc), ins, f"d_ypv{cc % 2}")

        def x2f_load(sync, s, e):
            i = NT + e
            wrec(sync, ("x2s", s, e))
            if e >= 2:
                wrec(sync, ("v_fin", s, e - 2))
            else:
                wrec(sync, ("v_n2", s, NT - 1))
            rdma(("x2f", s, e),
                 sync.dma_start(pf(X2RD + (i % 2) * C, C),
                                x2d[s][e * 128:(e + 1) * 128, :]),
                 f"d_x2l{i % 2}")

        W1CH = [(nm, (WQ, WK, WV)[k] + cc * 1024)
                for k, nm in enumerate(("wqT", "wkT", "wvT"))
                for cc in range(NCH)]
        W2ACH = [(nm, (WO, WR)[k] + cc * 1024)
                 for k, nm in enumerate(("woT", "wrT")) for cc in range(NCH)]
        W2BCH = [(nm, (WKF, WVF)[k] + cc * 1024)
                 for k, nm in enumerate(("wkfT", "wvfT")) for cc in range(NCH)]

        def w1_load(sync, i):
            nm, _col = W1CH[i]
            cc = i % NCH
            if i >= 2:
                wrec(sync, ("wc1", i - 2))
            rdma(("w1l", i),
                 sync.dma_start(pf(XROT + (i % 2) * C, C),
                                w_in[nm][cc * 128:(cc + 1) * 128, :]),
                 f"d_x{i % 2}")

        def w2_load(sync, i):
            nm, _col = (W2ACH + W2BCH)[i]
            cc = i % NCH
            if i >= 2:
                wrec(sync, ("wc2", i - 2))
            rdma(("w2l", i),
                 sync.dma_start(pf(X2RD + (i % 2) * C, C),
                                w_in[nm][cc * 128:(cc + 1) * 128, :]),
                 f"d_x2l{i % 2}")

        def f_sync(sync):
            for nm in ("bqc", "bkc", "bvc", "murc", "mukc"):
                inc(sync.dma_start(bvec[nm][:, :], vecs[nm][:, :]), "d_wh0", 16)
            inc(sync.dma_start(bob[:, :], bob_d[:, :]), "d_wh0", 16)
            rcmp(("wh",), "d_wh0")

            for i in range(NT):
                xa_load(sync, 0, i)
            for i in range(NT):
                xb_load(sync, 0, i)
            for t in range(NBLK):
                y1f_load(sync, 0, t)
            cbt_load(sync, 0)
            def emit_sorted(items):
                for _k, fn in sorted(items, key=lambda kv: kv[0]):
                    fn()

            items = [(-100.0, lambda: qq_load(sync, 0, 0)),
                     (-99.0, lambda: qq_load(sync, 0, 1))]
            for i in range(NT):
                items.append((-50.0 + i, lambda i=i: xa_load(sync, 1, i)))
            for i in range(NT):
                items.append((45.0 + i, lambda i=i: xb_load(sync, 1, i)))
            for lt in range(2, NT):
                items.append((32.0 * lt - 33,
                              lambda lt=lt: qq_load(sync, 0, lt)))
            for t in range(NBLK):
                k = max(pos_mark[("eM", 1, 24 * t)] - 6.0, 62.0)
                items.append((k, lambda t=t: y1f_load(sync, 1, t)))
            for t in range(NBLK):
                k = pos_mark[("eM", 0, 64 + 8 * t)] - 6.0
                items.append((k, lambda t=t: y1fv_load(sync, t)))
            items.append((pos_mark[("eM", 0, 95)] + 2.0,
                          lambda: cbt_load(sync, 1)))
            for fc in range(NCH):
                items.append((484.0 + 4 * fc, lambda fc=fc: kld_load(sync, fc)))
            emit_sorted(items)

            items = [(-100.0, lambda: qq_load(sync, 1, 0)),
                     (-99.0, lambda: qq_load(sync, 1, 1)),
                     (-98.0, lambda: vblk_load(sync, 0, 0))]
            for lt in range(2, NT):
                items.append((32.0 * lt - 33,
                              lambda lt=lt: qq_load(sync, 1, lt)))
            for t in range(1, NBLK):
                k = pos_mark[("eB", 0, t, 0)] - 512 - 10.0
                items.append((k, lambda t=t: vblk_load(sync, 0, t)))
            for i in range(NT):
                need = pos_mark[("eO", 0, min(i + 1, NT - 1))]
                if i % 4 == 3 and i // 4 + 1 < NBLK:
                    need = min(need, pos_mark[("eB", 0, i // 4 + 1, 0)])
                items.append((need - 512 - 10.0,
                              lambda i=i: xld_load(sync, 0, i)))
            fstart = pos_mark[("eF", 0, 0)] - 512
            for i in range(NT):
                items.append((fstart - 34.0 + 0.5 * i,
                              lambda i=i: x2l_load(sync, 0, i)))
            for t in range(NBLK):
                base = pos_mark[("eF", 0, 8 * t)] - 512
                if t == 0:
                    k = base - 10.0
                else:
                    k = base - 6.0
                items.append((k, lambda t=t: y2f_load(sync, 0, t)))
                for cc in range(NCH):
                    items.append((k + 0.5 + 0.1 * cc,
                                  lambda t=t, cc=cc: ypv_load(sync, 0, t, cc)))
            for e in range(NT):
                k = pos_mark[("eV", 0, e)] - 512 - 8.0
                items.append((k, lambda e=e: x2f_load(sync, 0, e)))
            for t in range(NBLK):
                k = pos_mark[("eB", 1, t, 0)] - 512 - 10.0
                items.append((k, lambda t=t: vblk_load(sync, 1, t)))
            for i in range(NT):
                need = pos_mark[("eO", 1, min(i + 1, NT - 1))]
                if i % 4 == 3 and i // 4 + 1 < NBLK:
                    need = min(need, pos_mark[("eB", 1, i // 4 + 1, 0)])
                items.append((need - 512 - 10.0,
                              lambda i=i: xld_load(sync, 1, i)))
            emit_sorted(items)
            for i in range(NT):
                x2l_load(sync, 1, i)
            for t in range(NBLK):
                y2f_load(sync, 1, t)
                for cc in range(NCH):
                    ypv_load(sync, 1, t, cc)
                for j in range(4):
                    x2f_load(sync, 1, 4 * t + j)

        def f_gpsimd(g):
            for w_key, col in (("wqT", WQ), ("wkT", WK), ("wvT", WV)):
                for cc in range(NCH):
                    inc(g.dma_start(pb(col + cc * 1024, 1024),
                                    w_in[w_key][cc * 128:(cc + 1) * 128, :]),
                        "d_w0", 16)
            rcmp(("w1",), "d_w0")
            g.iota(identi[:, :], [[1, 128]], channel_multiplier=-1,
                   allow_small_or_imprecise_dtypes=True)
            g.iota(es2i[:, :], [[1, 2], [0, 64]], channel_multiplier=-1,
                   allow_small_or_imprecise_dtypes=True)
            g.iota(eji[:, :], [[1, 16]], channel_multiplier=0,
                   allow_small_or_imprecise_dtypes=True)
            ins = g.iota(pi[:, :], [[0, 1]], channel_multiplier=1,
                         allow_small_or_imprecise_dtypes=True)
            inc(ins, "g_io", 1)
            wrec(g, ("vsetup",))
            for s in range(NSEQ):
                inc(g.dma_start(y2d[s][0:1, :], zrow), "d_z0", 16)
            rcmp(("zst",), "d_z0")

            for s in range(NSEQ):
                for i in range(NT):
                    wrec(g, ("v_nrm", s, i))
                    ins = g.dma_start(y1d[s][i * 128:(i + 1) * 128, :],
                                      pb(Y1ROT + (i % 4) * C, C))
                    rdma(("y1s", s, i), ins, f"d_y1s{i % 4}")
                if s == 0:
                    for t in range(NBLK):
                        for j in range(NCH):
                            o = 8 * t + j
                            wrec(g, ("v_ev", 0, 16 * t + 8 + j))
                            ins = g.dma_start(
                                qd[0][j * 128:(j + 1) * 128,
                                      t * 512:(t + 1) * 512],
                                pb(QVROT + (o % 4) * 512, 512))
                            rdma(("qv", 0, o), ins, f"d_qv{o % 4}")
                    for sl in range(4):
                        rcmp(("qs_all", sl), f"d_qv{sl}")
                else:
                    for t in range(NBLK):
                        for f in range(24):
                            o = 24 * t + f
                            wrec(g, ("v_ev", 1, o))
                            dst = (qd[1] if f < 8 else
                                   kd1 if f < 16 else vd[1])
                            fc = f % 8
                            ins = g.dma_start(
                                dst[fc * 128:(fc + 1) * 128,
                                    t * 512:(t + 1) * 512],
                                pb(QVROT + (o % 4) * 512, 512))
                            rdma(("qv", 1, o), ins, f"d_qv{o % 4}")
                    for sl in range(4):
                        rcmp(("qv_all", 1, sl), f"d_qv{sl}")

            wrec(g, ("t_mm", 1, 24 * NBLK - 1))
            for w_key, col in (("woT", WO), ("wrT", WR)):
                for cc in range(NCH):
                    inc(g.dma_start(pb(col + cc * 1024, 1024),
                                    w_in[w_key][cc * 128:(cc + 1) * 128, :]),
                        "d_w0", 16)
            rcmp(("w2a",), "d_w0")
            for t in range(NBLK):
                for j in range(NCH):
                    o = 32 + 8 * t + j
                    wrec(g, ("v_ev", 0, 64 + 8 * t + j))
                    ins = g.dma_start(
                        vd[0][j * 128:(j + 1) * 128, t * 512:(t + 1) * 512],
                        pb(QVROT + (o % 4) * 512, 512))
                    rdma(("qv", 0, o), ins, f"d_qv{o % 4}")
            wrec(g, ("t_mm", 0, 95))
            for w_key, col in (("wkfT", WKF), ("wvfT", WVF)):
                for cc in range(NCH):
                    inc(g.dma_start(pb(col + cc * 1024, 1024),
                                    w_in[w_key][cc * 128:(cc + 1) * 128, :]),
                        "d_w1", 16)
            rcmp(("w2b",), "d_w1")
            for sl in range(4):
                rcmp(("qv_all0", sl), f"d_qv{sl}")

            for s in range(NSEQ):
                for i in range(NT):
                    wrec(g, ("v_x2", s, i))
                    ins = g.dma_start(x2d[s][i * 128:(i + 1) * 128, :],
                                      pf(XLD + (i % 2) * C, C))
                    rdma(("x2s", s, i), ins, f"d_x2s{i % 2}")
                for sl in range(2):
                    rcmp(("x2s_all", s, sl), f"d_x2s{sl}")
                for i in range(NT):
                    wrec(g, ("v_n2", s, i))
                    ins = g.dma_start(y2d[s][1 + i * 128:1 + (i + 1) * 128, :],
                                      pb(Y2ROT + (i % 2) * C, C))
                    rdma(("y2s", s, i), ins, f"d_y2s{i % 2}")
                for sl in range(2):
                    rcmp(("y2s_all", s, sl), f"d_y2s{sl}")
                for e in range(NT):
                    wrec(g, ("v_fin", s, e))
                    ins = g.dma_start(out_d[s, e * 128:(e + 1) * 128, :],
                                      pf(TMP2 + (e % 2) * C, C))
                    rdma(("outst", s, e), ins, f"d_out{e % 2}")
            for sl in range(2):
                wge(g, f"d_out{sl}", tot[f"d_out{sl}"])

        def p1_tensor_entries(s):
            entries = []
            for t in range(NBLK):
                for f in range(24):
                    e = 24 * t + f

                    def emit(tn, t=t, f=f, e=e, s=s):
                        if f == 0:
                            wrec(tn, ("y1f", s, t))
                        if e >= 2:
                            wrec(tn, ("v_ev", s, e - 2))
                        col = (WQ, WK, WV)[f // 8]
                        fc = f % 8
                        for cc in range(NCH):
                            ins = tn.matmul(
                                pAB[:, (f % 2) * 512:(f % 2 + 1) * 512],
                                pb(col + cc * 1024 + fc * 128, 128),
                                pb(Y1F + cc * 512, 512),
                                start=(cc == 0), stop=(cc == NCH - 1))
                        inc(ins, "t_mm", 1)
                        rcmp(("t_mm", s, e), "t_mm")
                    entries.append([emit, 1.8, 0.0, ("eM", s, e)])

                    if 1 <= f <= 16:
                        fq = f - 1
                        gq = 16 * t + fq

                        def emitq(tn, t=t, fq=fq, gq=gq, s=s):
                            wrec(tn, ("v_ev", s, 24 * t + fq))
                            if fq in (0, 8):
                                wrec(tn, ("v_qn_prev", s, t))
                            dst = (pD[0:16, 0:512] if fq < 8
                                   else pD[0:16, 512:1024])
                            ins = tn.matmul(
                                dst, eselq[:, (fq % 8) * 16:(fq % 8 + 1) * 16],
                                pb(SQROT + (fq % 2) * 512, 512),
                                start=(fq % 8 == 0), stop=(fq % 8 == 7))
                            inc(ins, "t_qn", 1)
                            rcmp(("t_qn", s, gq), "t_qn")
                        entries.append([emitq, 0.3, 0.0, ("eQ", s, gq)])
            return entries

        def p1_kq_entries():
            entries = []
            for t in range(NBLK):
                for idx in range(16):
                    f = idx + 8 if idx < 8 else idx - 8
                    e = 16 * t + idx

                    def emit(tn, t=t, f=f, e=e, idx=idx):
                        if idx == 0:
                            wrec(tn, ("y1f", 0, t))
                            if t == 0:
                                wrec(tn, ("w1",))
                        if e >= 2:
                            wrec(tn, ("v_ev", 0, e - 2))
                        col = WK if idx < 8 else WQ
                        fc = f % 8
                        for cc in range(NCH):
                            ins = tn.matmul(
                                pAB[:, (e % 2) * 512:(e % 2 + 1) * 512],
                                pb(col + cc * 1024 + fc * 128, 128),
                                pb(Y1F + cc * 512, 512),
                                start=(cc == 0), stop=(cc == NCH - 1))
                        inc(ins, "t_mm", 1)
                        rcmp(("t_mm", 0, e), "t_mm")
                    entries.append([emit, 1.8, 0.0, ("eM", 0, e)])

                    def emitq(tn, t=t, idx=idx, e=e):
                        wrec(tn, ("v_ev", 0, e))
                        if idx in (0, 8):
                            wrec(tn, ("v_qn_prev", 0, t))
                        dst = (pD[0:16, 512:1024] if idx < 8
                               else pD[0:16, 0:512])
                        ins = tn.matmul(
                            dst, eselq[:, (idx % 8) * 16:(idx % 8 + 1) * 16],
                            pb(SQROT + (e % 2) * 512, 512),
                            start=(idx % 8 == 0), stop=(idx % 8 == 7))
                        inc(ins, "t_qn", 1)
                        rcmp(("t_qn", 0, e), "t_qn")
                    entries.append([emitq, 0.3, 0.0, ("eQ", 0, e)])
            return entries

        def p1_v0_entries():
            entries = []
            for t in range(NBLK):
                for j in range(NCH):
                    e = 64 + 8 * t + j

                    def emitv(tn, t=t, j=j, e=e):
                        if j == 0:
                            wrec(tn, ("y1fv", t))
                        if e == 64:
                            wrec(tn, ("v_ev", 1, 94))
                        elif e == 65:
                            wrec(tn, ("v_ev", 1, 95))
                        else:
                            wrec(tn, ("v_ev", 0, e - 2))
                        for cc in range(NCH):
                            ins = tn.matmul(
                                pAB[:, (e % 2) * 512:(e % 2 + 1) * 512],
                                pb(WV + cc * 1024 + j * 128, 128),
                                pb(Y1F + cc * 512, 512),
                                start=(cc == 0), stop=(cc == NCH - 1))
                        inc(ins, "t_mm", 1)
                        rcmp(("t_mm", 0, e), "t_mm")
                    entries.append([emitv, 1.8, 0.0, ("eM", 0, e)])
            return entries

        def p2_half(tn, s, lt, h, j):
            q = 32 * lt + 2 * h + j
            if h == 0 and j == 0:
                wrec(tn, ("qQ", s, lt))
                if lt == 0:
                    wrec(tn, ("cbias_f32", s))
            if s == 1 and lt == 0 and h == 0 and j == 0:
                wrec(tn, ("Kld",))
            if q >= 2:
                wrec(tn, ("s_ex", s, q - 2))
            elif s == 1:
                wrec(tn, ("s_ex", 0, NHALF - 2 + q))
            po = (h % 2) * 64
            hc = h // 2
            base = (lt % 2) * 1024 + hc * 128
            lhsT = qblk[po:po + 64, base:base + 128]
            for m in range(2):
                mm = 2 * j + m
                rhs = K_sb[po:po + 64, hc * T + mm * 512:hc * T + (mm + 1) * 512]
                ins = tn.matmul(pS[:, (q % 2) * 1024 + m * 512:
                                   (q % 2) * 1024 + (m + 1) * 512],
                                lhsT, rhs, start=True, stop=True)
            inc(ins, "t_sc", 1)
            rcmp(("t_sc_half", s, q), "t_sc")

        def p3_outproj_entries(s):
            entries = []
            for t in range(NBLK):
                def emitT(tn, t=t, s=s):
                    if t == 0:
                        wrec(tn, ("v_qn", 1 - s, NBLK - 1))
                        wrec(tn, ("v_qn", s, NBLK - 1))
                        if s == 1:
                            wrec(tn, ("v_kt", 0, NT - 1))
                    else:
                        wrec(tn, ("v_vp", s, 8 * (t - 1) + NCH - 1))
                    for jj in range(4):
                        lt = 4 * t + jj
                        wrec(tn, ("v_rs", s, lt))
                        if jj == 0:
                            wrec(tn, ("v_rsf_prev", s, t))
                        ins = tn.transpose(
                            pD[0:16, jj * 128:(jj + 1) * 128],
                            rs_f32[:, (s * NT + lt) * NH:(s * NT + lt + 1) * NH],
                            identi[:, :])
                    inc(ins, "t_rsT", 1)
                    rcmp(("t_rsT", s, t), "t_rsT")
                entries.append([emitT, 0.8, 0.0, ("eT", s, t)])

                for cc in range(NCH):
                    def emitB(tn, t=t, cc=cc, s=s):
                        wrec(tn, ("v_rsf", s, t))
                        if cc == 0 and t == 0:
                            wrec(tn, ("vb", s, 0))
                            wrec(tn, ("esel2",))
                        elif cc == 0:
                            wrec(tn, ("vb", s, t))
                        if 8 * t + cc >= 2:
                            wrec(tn, ("v_vp", s, 8 * t + cc - 2))
                        dst = (pD[:, 512:1024] if cc % 2 == 0
                               else pD[:, 0:512])
                        ins = tn.matmul(dst, pb(ESEL2 + cc * 128, 128, 0, 16),
                                        rsf_sb[0:16, :], start=True, stop=True)
                        inc(ins, "t_B", 1)
                        rcmp(("t_B", s, 8 * t + cc), "t_B")
                    entries.append([emitB, 0.3, 0.0, ("eB", s, t, cc)])

                for jj in range(4):
                    i = 4 * t + jj

                    def emitO(tn, t=t, jj=jj, i=i, s=s):
                        wrec(tn, ("v_vp", s, 8 * t + NCH - 1))
                        if s == 0 and i == 0:
                            wrec(tn, ("w2a",))
                        if i >= 1:
                            wrec(tn, ("v_x2", s, i - 1))
                        for cc in range(NCH):
                            lhsT = pb(VBLK + cc * 512 + jj * 128, 128)
                            for fh in range(2):
                                ins = tn.matmul(
                                    pAB[:, fh * 512:(fh + 1) * 512], lhsT,
                                    pb(WO + cc * 1024 + fh * 512, 512),
                                    start=(cc == 0), stop=(cc == NCH - 1))
                        inc(ins, "t_op", 1)
                        rcmp(("t_op", s, i), "t_op")
                    entries.append([emitO, 3.6, 0.0, ("eO", s, i)])
            return entries

        def p3_ffn_entries(s):
            entries = []
            for t in range(NBLK):
                for dp in range(NCH):
                    e = 8 * t + dp

                    def emitF(tn, t=t, dp=dp, e=e, s=s):
                        if dp == 0:
                            wrec(tn, ("v_mix", s, 8 * t + NCH - 1))
                            if s == 0 and t == 0:
                                wrec(tn, ("w2b",))
                        if e >= 2:
                            wrec(tn, ("v_ka", s, e - 2))
                        for cc in range(NCH):
                            ins = tn.matmul(
                                pAB[:, (dp % 2) * 512:(dp % 2 + 1) * 512],
                                pb(WKF + cc * 1024 + dp * 128, 128),
                                pb(XKc + cc * 512, 512),
                                start=(cc == 0), stop=(cc == NCH - 1))
                        inc(ins, "t_kf", 1)
                        rcmp(("t_kf", s, e), "t_kf")
                    entries.append([emitF, 1.8, 0.0, ("eF", s, e)])

                def emitR(tn, tt, t=t, s=s):
                    e = 4 * t + tt
                    if tt == 0:
                        wrec(tn, ("v_mix", s, 8 * t + NCH - 1))
                        if e == 0:
                            wrec(tn, ("v_vp", s, 8 * NBLK - 1))
                        else:
                            wrec(tn, ("v_kt", s, e - 1))
                    elif tt == 1:
                        wrec(tn, ("v_rt", s, e - 1))
                    else:
                        wrec(tn, ("v_kt", s, e - 2))
                    for cc in range(NCH):
                        lhsT = pb(Y2F + cc * 512 + tt * 128, 128)
                        for fh in range(2):
                            ins = tn.matmul(
                                pD[:, fh * 512:(fh + 1) * 512], lhsT,
                                pb(WR + cc * 1024 + fh * 512, 512),
                                start=(cc == 0), stop=(cc == NCH - 1))
                    inc(ins, "t_r", 1)
                    rcmp(("t_r", s, e), "t_r")

                def emitV(tn, tt, t=t, s=s):
                    e = 4 * t + tt
                    if tt == 0:
                        wrec(tn, ("v_ka", s, 8 * t + NCH - 1))
                    if tt < 3:
                        wrec(tn, ("v_rt", s, e + 1))
                    else:
                        wrec(tn, ("v_kt", s, e - 1))
                    for dp in range(NCH):
                        lhsT = pb(KACT + dp * 512 + tt * 128, 128)
                        for fh in range(2):
                            ins = tn.matmul(
                                pD[:, fh * 512:(fh + 1) * 512], lhsT,
                                pb(WVF + dp * 1024 + fh * 512, 512),
                                start=(dp == 0), stop=(dp == NCH - 1))
                    inc(ins, "t_kv", 1)
                    rcmp(("t_kv", s, e), "t_kv")

                def mkR(tt, er=emitR):
                    return [lambda tn, tt=tt, er=er: er(tn, tt), 3.6, 0.0,
                            ("eR", s, 4 * t + tt)]

                def mkV(tt, ev=emitV):
                    return [lambda tn, tt=tt, ev=ev: ev(tn, tt), 3.6, 0.0,
                            ("eV", s, 4 * t + tt)]
                entries += [mkR(0), mkR(1), mkV(0), mkR(2), mkV(1), mkR(3),
                            mkV(2), mkV(3)]
            return entries

        schedule = []
        pos_mark = {}
        sq2shared = [128]
        nrs_early = [2]

        def assemble():
            schedule.clear()
            pos_mark.clear()
            half_idx = [0]

            def pull(queue, budget):
                while (queue and budget >= queue[0][1]
                       and half_idx[0] >= queue[0][2]):
                    fn, cost, _mh, name = queue.pop(0)
                    schedule.append(fn)
                    pos_mark[name] = half_idx[0]
                    if name == ("eO", 0, 15) and queue:
                        queue[0][2] = half_idx[0] + 44
                    if name == ("eR", 0, 15):
                        for q in queue:
                            if q[3] == ("eB", 1, 0, 0):
                                q[2] = half_idx[0] + 16
                                break
                    budget -= cost
                return budget

            for fn, _c, _m, name in p1_kq_entries():
                schedule.append(fn)
                pos_mark[name] = 0
            for s in range(NSEQ):
                if s == 0:
                    partner = p1_tensor_entries(1) + p1_v0_entries()
                    partner[0][2] = 72.0
                else:
                    partner = (p3_outproj_entries(0) + p3_ffn_entries(0) +
                               p3_outproj_entries(1))
                    partner[0][2] = 512 + 8.0
                debt = 0.0
                for lt in range(NT):
                    for h in range(NH):
                        for j in range(2):
                            def mk(s=s, lt=lt, h=h, j=j):
                                return lambda tn: p2_half(tn, s, lt, h, j)
                            schedule.append(mk())
                            half_idx[0] = 32 * lt + 2 * h + j + s * NHALF
                            debt = pull(partner, min(debt + 0.62, 6.0))
                while partner:
                    fn, _c, _m, name = partner.pop(0)
                    half_idx[0] += _c / 0.62
                    schedule.append(fn)
                    pos_mark[name] = half_idx[0]
            for fn, _c, _m, name in p3_ffn_entries(1):
                half_idx[0] += _c / 0.62
                schedule.append(fn)
                pos_mark[name] = half_idx[0]

        def f_tensor(tn):
            for fn in schedule:
                fn(tn)

        def f_vector(v):
            wge(v, "g_io", 1)
            wrec(v, ("wh",))
            v.tensor_single_scalar(identi[:, :], identi[:, :], 0.0,
                                   ALU.is_equal)
            v.drain()
            v.tensor_single_scalar(pg[:, :], pi[:, :], 63.5, ALU.is_gt)
            v.drain()
            for cc in range(NCH):
                v.tensor_scalar(eqt[:, :], eji[:, :], float(-2 * cc), None,
                                ALU.add)
                v.drain()
                v.tensor_scalar(eselq[:, cc * 16:(cc + 1) * 16], eqt[:, :],
                                pg[:, 0:1], None, ALU.is_equal)
                v.drain()
            v.memset(epsv[:, :], EPS)
            v.drain()
            v.memset(zrow, 0.0)
            dinc(v, "v_set")
            rcmp(("vsetup",), "v_set")

            def ln_stats(s):
                for i in range(NT):
                    wrec(v, ("xA", s, i))
                    xt = pf(XROT + (i % 2) * C, C)
                    v.reduce_sum(sums[:, i:i + 1], xt, axis=AXX)
                    v.scalar_tensor_tensor(sdump[:, :], xt, 1.0, xt,
                                           ALU.bypass, ALU.mult,
                                           accum_out=ssqs[:, i:i + 1])
                    dinc(v, "v_st")
                    rcmp(("v_st", s, i), "v_st")

            def ln_batch(key):
                v.tensor_scalar(meanv[:, :], sums[:, :], 1.0 / C, None,
                                ALU.mult)
                v.tensor_scalar(varv[:, :], ssqs[:, :], 1.0 / C, None,
                                ALU.mult)
                v.drain()
                v.tensor_mul(m2t[:, :], meanv[:, :], meanv[:, :])
                v.drain()
                v.tensor_sub(varv[:, :], varv[:, :], m2t[:, :])
                dinc(v, "v_ti")
                rcmp(key, "v_ti")

            def ln_inv(skey):
                wrec(v, skey)
                v.reciprocal(invv[:, :], stdv[:, :])
                v.drain()
                v.scalar_tensor_tensor(negmi[:, :], meanv[:, :], -1.0,
                                       invv[:, :], ALU.mult, ALU.mult)
                dinc(v, "v_ti")

            def p1_norm(s):
                for i in range(NT):
                    wrec(v, ("xB", s, i))
                    if s == 0 and i == 0:
                        wrec(v, ("zst",))
                    if i >= 4:
                        wrec(v, ("y1s", s, i - 4))
                    elif s == 1:
                        wrec(v, ("y1s", 0, NT - 4 + i))
                    v.tensor_scalar(pb(Y1ROT + (i % 4) * C, C),
                                    pf(XROT + (i % 2) * C, C),
                                    invv[:, i:i + 1], negmi[:, i:i + 1],
                                    ALU.mult, ALU.add)
                    dinc(v, "v_nrm")
                    rcmp(("v_nrm", s, i), "v_nrm")

            def cbias_build(s):
                v.reduce_max(kmax2[0:16, 0:1], qkn_sb[32:48, :], axis=AXX)
                v.drain()
                v.tensor_scalar(negk16[0:16, 0:1], kmax2[0:16, 0:1],
                                -1.0 / 16.0, None, ALU.mult)
                v.drain()
                v.tensor_scalar(qkn_sb[0:16, :], qkn_sb[0:16, :], -1.0 / 16.0,
                                negk16[0:16, 0:1], ALU.mult, ALU.add)
                dinc(v, "v_cb")
                rcmp(("v_cb", s), "v_cb")
                wrec(v, ("cbT", s))
                v.tensor_copy(cbias[:, s * 256:(s + 1) * 256],
                              cbias_bf[:, s * 256:(s + 1) * 256])
                dinc(v, "v_cb")
                rcmp(("cbias_f32", s), "v_cb")

            def p1_evacs_kq():
                rec[("v_qn_prev", 0, 0)] = ("v_qn", tot["v_qn"])
                for t in range(NBLK):
                    for idx in range(16):
                        e = 16 * t + idx
                        f = idx + 8 if idx < 8 else idx - 8
                        wrec(v, ("t_mm", 0, e))
                        bias = bvec["bkc"] if idx < 8 else bvec["bqc"]
                        fc = f % 8
                        src = pAB[:, (e % 2) * 512:(e % 2 + 1) * 512]
                        if idx < 8:
                            dstt = K_sb[:, fc * T + t * 512:
                                        fc * T + (t + 1) * 512]
                        else:
                            o = 8 * t + idx - 8
                            if o >= 4:
                                wrec(v, ("qv", 0, o - 4))
                            dstt = pb(QVROT + (o % 4) * 512, 512)
                        v.tensor_scalar(dstt, src, bias[:, fc:fc + 1], None,
                                        ALU.add)
                        if e >= 2:
                            wrec(v, ("t_qn", 0, e - 2))
                        v.drain()
                        v.tensor_mul(pb(SQROT + (e % 2) * 512, 512),
                                     dstt, dstt)
                        dinc(v, "v_ev")
                        rcmp(("v_ev", 0, e), "v_ev")
                    wrec(v, ("t_qn", 0, 16 * t + 7))
                    v.tensor_copy(qkn_sb[32:48, t * 512:(t + 1) * 512],
                                  pD[0:16, 512:1024])
                    v.drain()
                    wrec(v, ("t_qn", 0, 16 * t + 15))
                    v.tensor_copy(qkn_sb[0:16, t * 512:(t + 1) * 512],
                                  pD[0:16, 0:512])
                    dinc(v, "v_qn")
                    rcmp(("v_qn", 0, t), "v_qn")
                    rec[("v_qn_prev", 0, t + 1)] = ("v_qn", tot["v_qn"])
                cbias_build(0)

            def p1_evacs_v0():
                for t in range(NBLK):
                    for j in range(NCH):
                        e = 64 + 8 * t + j
                        o = 32 + 8 * t + j
                        wrec(v, ("t_mm", 0, e))
                        if o - 32 < 4:
                            wrec(v, ("qv", 1, 92 + o - 32))
                        else:
                            wrec(v, ("qv", 0, o - 4))
                        v.tensor_scalar(pb(QVROT + (o % 4) * 512, 512),
                                        pAB[:, (e % 2) * 512:
                                            (e % 2 + 1) * 512],
                                        bvec["bvc"][:, j:j + 1], None, ALU.add)
                        dinc(v, "v_ev")
                        rcmp(("v_ev", 0, e), "v_ev")

            def p1_evacs(s):
                rec[("v_qn_prev", s, 0)] = ("v_qn", tot["v_qn"])
                for t in range(NBLK):
                    for f in range(24):
                        e = 24 * t + f
                        wrec(v, ("t_mm", s, e))
                        bias = (bvec["bqc"] if f < 8 else
                                bvec["bkc"] if f < 16 else bvec["bvc"])
                        fc = f % 8
                        src = pAB[:, (f % 2) * 512:(f % 2 + 1) * 512]
                        o = 24 * t + f
                        if o >= 4:
                            wrec(v, ("qv", s, o - 4))
                        else:
                            wrec(v, ("qv", 0, 28 + o))
                        dstt = pb(QVROT + (o % 4) * 512, 512)
                        v.tensor_scalar(dstt, src, bias[:, fc:fc + 1], None,
                                        ALU.add)
                        if f < 16:
                            gq = 16 * t + f
                            if gq >= 2:
                                wrec(v, ("t_qn", s, gq - 2))
                            v.drain()
                            v.tensor_mul(pb(SQROT + (f % 2) * 512, 512),
                                         dstt, dstt)
                        dinc(v, "v_ev")
                        rcmp(("v_ev", s, e), "v_ev")
                    wrec(v, ("t_qn", s, 16 * t + 7))
                    v.tensor_copy(qkn_sb[0:16, t * 512:(t + 1) * 512],
                                  pD[0:16, 0:512])
                    v.drain()
                    wrec(v, ("t_qn", s, 16 * t + 15))
                    v.tensor_copy(qkn_sb[32:48, t * 512:(t + 1) * 512],
                                  pD[0:16, 512:1024])
                    dinc(v, "v_qn")
                    rcmp(("v_qn", s, t), "v_qn")
                    rec[("v_qn_prev", s, t + 1)] = ("v_qn", tot["v_qn"])

            def p2_rs(s, lt):
                wrec(v, ("s_exd", s, lt))
                base = s * NHALF + 32 * lt
                v.tensor_add(rs16[:, :], denom[:, base:base + 16],
                             denom[:, base + 16:base + 32])
                v.drain()
                v.reciprocal(rcp16[:, :], rs16[:, :])
                v.drain()
                v.tensor_mul(rs_f32[:, (s * NT + lt) * NH:
                                    (s * NT + lt + 1) * NH],
                             rs16[:, :], rcp16[:, :])
                dinc(v, "v_rs")
                rcmp(("v_rs", s, lt), "v_rs")

            def p3_partA(s, rs_cb=None, end_cb=None):
                if s == 0:
                    for cc in range(NCH):
                        v.tensor_single_scalar(pb(ESEL2 + cc * 128, 128, 0, 16),
                                               es2i[:, :], float(-2 * cc),
                                               ALU.is_equal)
                        v.drain()
                    dinc(v, "v_set")
                    rcmp(("esel2",), "v_set")
                rec[("v_rsf_prev", s, 0)] = ("v_rsf", tot["v_rsf"])
                for t in range(NBLK):
                    if rs_cb is not None:
                        rs_cb(t)
                    wrec(v, ("t_rsT", s, t))
                    if t >= 1:
                        wrec(v, ("t_B", s, 8 * (t - 1) + 7))
                    elif s == 1:
                        wrec(v, ("t_B", 0, 8 * NBLK - 1))
                    v.tensor_copy(rsf_sb[0:16, :], pD[0:16, 0:512])
                    dinc(v, "v_rsf")
                    rcmp(("v_rsf", s, t), "v_rsf")
                    rec[("v_rsf_prev", s, t + 1)] = ("v_rsf", tot["v_rsf"])
                    for cc in range(NCH):
                        wrec(v, ("t_B", s, 8 * t + cc))
                        sl = pb(VBLK + cc * 512, 512)
                        src = (pD[:, 512:1024] if cc % 2 == 0
                               else pD[:, 0:512])
                        v.tensor_mul(sl, sl, src)
                        dinc(v, "v_vp")
                        rcmp(("v_vp", s, 8 * t + cc), "v_vp")
                    for jj in range(4):
                        i = 4 * t + jj
                        wrec(v, ("t_op", s, i))
                        wrec(v, ("xld", s, i))
                        if s == 0 and i == 0:
                            wrec(v, ("wh",))
                        xt = pf(XLD + (i % 2) * C, C)
                        v.tensor_add(xt, xt, bob[:, :])
                        v.drain()
                        v.tensor_add(xt, xt, pAB[:, :])
                        v.drain()
                        v.reduce_sum(sums[:, i:i + 1], xt, axis=AXX)
                        v.scalar_tensor_tensor(sdump[:, :], xt, 1.0, xt,
                                               ALU.bypass, ALU.mult,
                                               accum_out=ssqs[:, i:i + 1])
                        dinc(v, "v_x2")
                        rcmp(("v_x2", s, i), "v_x2")
                    if end_cb is not None:
                        end_cb(t)

            def fin_one(s, e):
                wrec(v, ("s_th", s, e))
                wrec(v, ("x2f", s, e))
                if e >= 2:
                    wrec(v, ("outst", s, e - 2))
                elif s == 1:
                    wrec(v, ("outst", 0, NT - 2 + e))
                ts_ = pf(RT + (e % 2) * C, C)
                os_ = pf(TMP2 + (e % 2) * C, C)
                v.tensor_scalar(ts_, ts_, 0.5, 0.5, ALU.mult, ALU.add)
                v.drain()
                v.tensor_mul(os_, ts_, pf(KTMP + (e % 2) * C, C))
                v.drain()
                v.tensor_add(os_, os_, pf(X2RD + (e % 2) * C, C))
                dinc(v, "v_fin")
                rcmp(("v_fin", s, e), "v_fin")

            def p3_partB(s):
                ln_batch(("v_ti_ln2", s))
                ln_inv(("s_sq", s, "ln2"))
                for i in range(NT):
                    wrec(v, ("x2l", s, i))
                    if i >= 2:
                        wrec(v, ("y2s", s, i - 2))
                    v.tensor_scalar(pb(Y2ROT + (i % 2) * C, C),
                                    pf(X2RD + (i % 2) * C, C),
                                    invv[:, i:i + 1], negmi[:, i:i + 1],
                                    ALU.mult, ALU.add)
                    dinc(v, "v_n2")
                    rcmp(("v_n2", s, i), "v_n2")
                for t in range(NBLK):
                    for cc in range(NCH):
                        wrec(v, ("y2f", s, t))
                        wrec(v, ("ypv", s, t, cc))
                        if t >= 1 and cc == 0:
                            wrec(v, ("t_kf", s, 8 * (t - 1) + NCH - 1))
                        yf = pb(Y2F + cc * 512, 512)
                        yp = pb(YPV + (cc % 2) * 512, 512)
                        v.tensor_sub(yf, yf, yp)
                        v.drain()
                        v.scalar_tensor_tensor(pb(XKc + cc * 512, 512), yf,
                                               bvec["mukc"][:, cc:cc + 1],
                                               yp, ALU.mult, ALU.add)
                        v.drain()
                        v.scalar_tensor_tensor(yf, yf,
                                               bvec["murc"][:, cc:cc + 1],
                                               yp, ALU.mult, ALU.add)
                        dinc(v, "v_mix")
                        rcmp(("v_mix", s, 8 * t + cc), "v_mix")
                    for dp in range(NCH):
                        e = 8 * t + dp
                        wrec(v, ("t_kf", s, e))
                        if t >= 1 and dp == 0:
                            wrec(v, ("t_kv", s, 4 * (t - 1) + 3))
                        src = pAB[:, (dp % 2) * 512:(dp % 2 + 1) * 512]
                        v.tensor_scalar(sdump[:, 0:512], src, 0.0, None,
                                        ALU.max)
                        v.drain()
                        v.tensor_mul(pb(KACT + dp * 512, 512),
                                     sdump[:, 0:512], sdump[:, 0:512])
                        dinc(v, "v_ka")
                        rcmp(("v_ka", s, e), "v_ka")
                    b = 4 * t
                    for kind, e in (("rt", b), ("fin", b - 1), ("rt", b + 1),
                                    ("kt", b), ("fin", b), ("rt", b + 2),
                                    ("kt", b + 1), ("fin", b + 1),
                                    ("rt", b + 3), ("kt", b + 2),
                                    ("fin", b + 2), ("kt", b + 3)):
                        if kind == "fin":
                            if e >= 0:
                                fin_one(s, e)
                        elif kind == "rt":
                            wrec(v, ("t_r", s, e))
                            if e >= 2:
                                wrec(v, ("s_th", s, e - 2))
                            v.tensor_copy(pf(RT + (e % 2) * C, C), pD[:, :])
                            dinc(v, "v_rt")
                            rcmp(("v_rt", s, e), "v_rt")
                        else:
                            wrec(v, ("t_kv", s, e))
                            v.tensor_copy(pf(KTMP + (e % 2) * C, C), pD[:, :])
                            dinc(v, "v_kt")
                            rcmp(("v_kt", s, e), "v_kt")
                if True:
                    fin_one(s, NT - 1)

            ln_stats(0)
            ln_batch(("v_ti_ln1", 0))
            ln_inv(("s_sq", 0, "ln1"))
            p1_norm(0)
            p1_evacs_kq()
            ln_stats(1)
            ln_batch(("v_ti_ln1", 1))
            ln_inv(("s_sq", 1, "ln1"))
            p1_norm(1)
            p1_evacs(1)
            p1_evacs_v0()
            cbias_build(1)
            for lt in range(NT):
                p2_rs(0, lt)
            dinc(v, "v_set")
            rcmp(("p1_done_v", 1), "v_set")
            p3_partA(0)
            for lt in range(nrs_early[0]):
                p2_rs(1, lt)
            p3_partB(0)
            nxt = [nrs_early[0]]

            def rs_cb(t):
                while nxt[0] <= 4 * t + 3:
                    p2_rs(1, nxt[0])
                    nxt[0] += 1
            p3_partA(1, rs_cb)
            while nxt[0] < NT:
                p2_rs(1, nxt[0])
                nxt[0] += 1
            p3_partB(1)

        def f_scalar(sc):
            def sqrtln(s, which):
                wrec(sc, (("v_ti_ln1", s) if which == "ln1"
                          else ("v_ti_ln2", s)))
                sc.activation(stdv[:, :], varv[:, :], AF.Sqrt,
                              bias=epsv[:, 0:1])
                dinc(sc, "s_sq")
                rcmp(("s_sq", s, which), "s_sq")

            def tanh_one(s, e):
                wrec(sc, ("v_rt", s, e))
                sc.activation(pf(RT + (e % 2) * C, C), pf(RT + (e % 2) * C, C),
                              AF.Tanh, scale=0.5)
                dinc(sc, "s_th")
                rcmp(("s_th", s, e), "s_th")

            def exps(s, inserts):
                for lt in range(NT):
                    for h in range(NH):
                        for j in range(2):
                            q = 32 * lt + 2 * h + j
                            for fn in inserts.pop(q, []):
                                fn()
                            wrec(sc, ("t_sc_half", s, q))
                            ins = sc.activation(
                                dump8[:, (q % 4) * 1024:(q % 4 + 1) * 1024],
                                pS[:, (q % 2) * 1024:(q % 2 + 1) * 1024],
                                AF.Exp,
                                bias=cbias[:, (s * NT + lt) * NH + h:
                                           (s * NT + lt) * NH + h + 1],
                                scale=SCALE,
                                accum_out=denom[:, s * NHALF + 32 * lt +
                                                16 * j + h:
                                                s * NHALF + 32 * lt +
                                                16 * j + h + 1])
                            inc(ins, "s_ex", 1)
                            rcmp(("s_ex", s, q), "s_ex")
                            if q % 4 == 3:
                                sc.drain()
                    sc.drain(fusable=True).then_inc(sems["s_exd"], 1)
                    tot["s_exd"] += 1
                    rcmp(("s_exd", s, lt), "s_exd")

            sqrtln(0, "ln1")
            exps(0, {36: [lambda: sqrtln(1, "ln1")]})
            ins1 = {}
            sq2pos = sq2shared[0]
            ins1.setdefault(sq2pos, []).append(lambda: sqrtln(0, "ln2"))
            done_tanh = [0]
            prevq = sq2pos
            for e in range(NT):
                p = pos_mark[("eR", 0, e)]
                qpos = min(max(p - 512 + 8, prevq + 2), 508)
                prevq = qpos
                ins1.setdefault(qpos, []).append(
                    (lambda e=e: (tanh_one(0, e),
                                  done_tanh.__setitem__(0, e + 1))))
            exps(1, ins1)
            while done_tanh[0] < NT:
                tanh_one(0, done_tanh[0])
                done_tanh[0] += 1
            sqrtln(1, "ln2")
            for e in range(NT):
                tanh_one(1, e)

        assemble()
        sq2shared[0] = max(min(pos_mark[("eO", 0, 15)] - 512 + 8, 440), 128)
        nrs_early[0] = max(0, min(NT, (sq2shared[0] - 32) // 32 + 1))
        sections = (f_sync, f_gpsimd, f_vector, f_scalar, f_tensor)
        stub = _Stub()
        for f in sections:
            f(stub)
        for nm in tot:
            tot[nm] = 0
        dry[0] = False

        blk = st.enter_context(nc.Block())

        @blk.sync
        def _(sync):
            f_sync(sync)

        @blk.gpsimd
        def _(g):
            f_gpsimd(g)

        @blk.vector
        def _(v):
            f_vector(v)

        @blk.scalar
        def _(sc):
            f_scalar(sc)

        @blk.tensor
        def _(tn):
            f_tensor(tn)

    return nc


def _prep_inputs(L0_attn_x, L0_attn_w_in, L0_attn_b_in, L0_attn_w_out,
                 L0_attn_b_out, L0_ffn_W_r, L0_ffn_W_k, L0_ffn_W_v,
                 L0_ffn_mu_r, L0_ffn_mu_k):
    x = np.ascontiguousarray(np.asarray(L0_attn_x, dtype=np.float32))
    w3 = np.asarray(L0_attn_w_in, dtype=np.float32).reshape(NH, 3, DH, C)
    wq = w3[:, 0].reshape(C, C)
    wk = w3[:, 1].reshape(C, C)
    wv = w3[:, 2].reshape(C, C)
    b3 = np.asarray(L0_attn_b_in, dtype=np.float32).reshape(NH, 3, DH)
    bq, bk, bv = (b3[:, i].reshape(C) for i in range(3))

    def colv(v):
        return np.ascontiguousarray(np.asarray(v, np.float32).reshape(8, 128).T)

    common = {
        "wqT": np.ascontiguousarray(wq.T),
        "wkT": np.ascontiguousarray(wk.T),
        "wvT": np.ascontiguousarray(wv.T),
        "woT": np.ascontiguousarray(np.asarray(L0_attn_w_out, np.float32).T),
        "wrT": np.ascontiguousarray(np.asarray(L0_ffn_W_r, np.float32).T),
        "wkfT": np.ascontiguousarray(np.asarray(L0_ffn_W_k, np.float32).T),
        "wvfT": np.ascontiguousarray(np.asarray(L0_ffn_W_v, np.float32).T),
        "bqc": colv(bq), "bkc": colv(bk), "bvc": colv(bv),
        "murc": colv(L0_ffn_mu_r), "mukc": colv(L0_ffn_mu_k),
        "bob": np.ascontiguousarray(
            np.broadcast_to(np.asarray(L0_attn_b_out, np.float32), (128, C))),
    }
    in_maps = []
    for core in range(8):
        m = dict(common)
        m["x"] = np.ascontiguousarray(x[2 * core:2 * core + 2])
        in_maps.append(m)
    return in_maps


def kernel(**inputs):
    in_maps = _prep_inputs(**inputs)
    nc = build()
    res = run_bass_kernel_spmd(nc, in_maps, core_ids=list(range(8)))
    return np.concatenate([r["out"] for r in res.results], axis=0)


if __name__ == "__main__":
    nc = build()
    print("build ok")
